# version 03
# baseline (speedup 1.0000x reference)
import math
from contextlib import ExitStack

import numpy as np
import ml_dtypes

import concourse.bass as bass
import concourse.mybir as mybir
from concourse.bass_utils import run_bass_kernel_spmd

F32 = mybir.dt.float32
BF16 = mybir.dt.bfloat16
I32 = mybir.dt.int32
AF = mybir.ActivationFunctionType
ALU = mybir.AluOpType
AX = mybir.AxisListType.X

B, N, D = 2, 4096, 4096
H, HD = 32, 128
II = 11008
T = 1024
TC = T // 128
DC = D // 128
ICN = II // 128
RS = 2 * D
INV_SQRT_HD = 1.0 / math.sqrt(HD)
ENGS = ("sync", "tensor", "vector", "scalar", "gpsimd")
LN_EPS = 1e-5
RMS_EPS = 1e-6


class Sem:
    __slots__ = ("h", "n")

    def __init__(self, h):
        self.h = h
        self.n = 0


class Builder:
    def __init__(self):
        self.nc = bass.Bass("TRN2", target_bir_lowering=False)
        self.q = {e: [] for e in ENGS}
        self.ctx = ExitStack()
        self.bar_k = 0

    def sem(self, name):
        return Sem(self.ctx.enter_context(self.nc.semaphore(name)))

    def semlist(self, name, k):
        return [self.sem(f"{name}{i}") for i in range(k)]

    def sbuf(self, name, shape, dtype):
        return self.ctx.enter_context(self.nc.sbuf_tensor(name, shape, dtype))

    def emit(self, eng, fn):
        self.q[eng].append(fn)

    def wait(self, eng, sem, val=None):
        v = sem.n if val is None else val
        if v > 0:
            self.emit(eng, lambda e, s=sem.h, vv=v: e.wait_ge(s, vv))

    SELF_SYNC = ("vector", "scalar", "gpsimd")

    def run(self, eng, build, sem=None, k=1):
        if eng in self.SELF_SYNC and sem is not None and sem.n > 0:
            self.wait(eng, sem, sem.n)
        if sem is not None:
            sem.n += k
        def f(e, bld=build, s=sem, kk=k):
            ins = bld(e)
            if s is not None:
                ins.then_inc(s.h, kk)
        self.emit(eng, f)
        return sem.n if sem is not None else None

    def dma(self, eng, out, in_, sem, **kw):
        return self.run(eng, lambda e, o=out, i=in_, k=kw: e.dma_start(out=o, in_=i, **k),
                        sem, 16)

    def barrier(self, drain=True):
        self.bar_k += 1
        tgt = 5 * self.bar_k
        for eng in ENGS:
            if drain and eng in ("tensor", "vector", "scalar", "gpsimd"):
                self.emit(eng, lambda e: e.drain())
            self.emit(eng, lambda e, s=self.bar.h: e.sem_inc(s, 1))
            self.emit(eng, lambda e, s=self.bar.h, v=tgt: e.wait_ge(s, v))

    def finalize_block(self):
        nc = self.nc
        with nc.Block() as blk:
            for eng in ENGS:
                fns = self.q[eng]
                getattr(blk, eng)(lambda e, fns=fns: [fn(e) for fn in fns])


def build_program():
    b = Builder()
    nc = b.nc

    def din(name, shape, dtype):
        return nc.dram_tensor(name, shape, dtype, kind="ExternalInput").ap()

    x_in = din("x_in", [T, D], F32)
    wq = din("wq", [D, D], F32)
    wk = din("wk", [D, D], F32)
    wv = din("wv", [D, D], F32)
    wo = din("wo", [D, D], F32)
    qwg = din("qwg", [II, D], I32)
    qwu = din("qwu", [II, D], I32)
    qwd = din("qwd", [D, II], I32)
    sg_in = din("sg", [128, ICN * 32], F32)
    zg_in = din("zg", [128, ICN * 32], F32)
    su_in = din("su", [128, ICN * 32], F32)
    zu_in = din("zu", [128, ICN * 32], F32)
    sd_in = din("sd", [128, 32 * ICN], F32)
    zd_in = din("zd", [128, 32 * ICN], F32)
    bq_in = din("bq_p", [128, 32], F32)
    bk_in = din("bk_p", [128, 32], F32)
    bv_in = din("bv_p", [128, 32], F32)
    bo_in = din("bo_p", [128, 32], F32)
    gq_in = din("gq_c", [128, 1], F32)
    gk_in = din("gk_c", [128, 1], F32)
    id_in = din("ident_bf", [128, 128], BF16)
    idf_in = din("ident_f", [128, 128], F32)
    onb_in = din("ones_bf", [128, 128], BF16)
    onf_in = din("ones_f", [128, 128], F32)

    y_out = nc.dram_tensor("y", [T, D], F32, kind="ExternalOutput").ap()

    cc_in = nc.dram_tensor("cc_in", [RS, 1024], BF16).ap()
    cc_out = nc.dram_tensor("cc_out", [4 * RS, 1024], BF16).ap()
    q_dram = nc.dram_tensor("q_dram", [D, T], BF16).ap()
    x1_dram = nc.dram_tensor("x1_dram", [T, D], F32).ap()
    hact_dram = nc.dram_tensor("hact_dram", [II, T], BF16).ap()
    wdt_dram = nc.dram_tensor("wdt_dram", [32, 128, II], BF16).ap()

    b.bar = b.sem("bar")
    ld_sy = b.sem("ld_sy")
    s_pe = b.sem("s_pe")
    s_v = b.sem("s_v")
    s_s = b.sem("s_s")
    s_gp = b.sem("s_gp")
    s_cc = b.sem("s_cc")
    ldx = b.semlist("ldx", 2)
    ldw = b.semlist("ldw", 2)
    stk = b.semlist("stk", 2)
    stv = b.semlist("stv", 4)
    ldk = b.semlist("ldk", 2)
    ldv = b.semlist("ldv", 2)
    ldq = b.semlist("ldq", 2)
    ldx4 = b.semlist("ldx4", 4)
    stx1 = b.semlist("stx1", 4)
    ldqg = b.semlist("ldqg", 2)
    ldqu = b.semlist("ldqu", 2)
    sth = b.semlist("sth", 2)
    ldqd = b.semlist("ldqd", 2)
    stw = b.semlist("stw", 2)
    ldb = b.semlist("ldb", 2)
    ldx1b = b.sem("ldx1b")
    sty4 = b.semlist("sty4", 4)

    ident = b.sbuf("ident_sb", [128, 128], BF16)
    ones_b = b.sbuf("ones_b_sb", [128, 128], BF16)
    ones_f = b.sbuf("ones_f_sb", [128, 128], F32)
    bq_sb = b.sbuf("bq_sb", [128, 32], F32)
    bk_sb = b.sbuf("bk_sb", [128, 32], F32)
    bv_sb = b.sbuf("bv_sb", [128, 32], F32)
    bo_sb = b.sbuf("bo_sb", [128, 32], F32)
    gq_sb = b.sbuf("gq_sb", [128, 1], F32)
    gk_sb = b.sbuf("gk_sb", [128, 1], F32)
    epsc = b.sbuf("epsc", [128, 2], F32)
    idf_in_sb = b.sbuf("idf_sb", [128, 128], F32)

    for dst, src in ((ident, id_in), (ones_b, onb_in), (ones_f, onf_in),
                     (bq_sb, bq_in), (bk_sb, bk_in), (bv_sb, bv_in),
                     (bo_sb, bo_in), (gk_sb, gk_in), (gq_sb, gq_in),
                     (idf_in_sb, idf_in)):
        b.dma("sync", dst[:], src, ld_sy)
    b.wait("vector", ld_sy)
    b.run("vector", lambda e: e.tensor_scalar(gq_sb[:], gq_sb[:], INV_SQRT_HD,
                                              None, ALU.mult), s_v)
    b.run("vector", lambda e: e.memset(epsc[:, 0:1], LN_EPS), s_v)
    b.run("vector", lambda e: e.memset(epsc[:, 1:2], RMS_EPS), s_v)
    b.barrier()

    def ln_transpose(src_dram, xnT_dst, sc, pre_wait=None):
        xt, xsq, xnb, ps = sc["xt"], sc["xsq"], sc["xnb"], sc["ps"]
        cols = sc["cols"]
        xnT_v = xnT_dst[:].rearrange("p (f t) -> p f t", f=DC)
        pe_done, ev_done, v_apply = [], [], []
        for tau in range(TC):
            par = tau % 2
            if tau == 0 and pre_wait is not None:
                for sem_, val_ in pre_wait:
                    b.wait("sync", sem_, val_)
            if tau >= 2:
                b.wait("sync", s_v, v_apply[tau - 2])
            b.dma("sync", xt[par][:], src_dram[tau * 128:(tau + 1) * 128, :],
                  ldx[par])
            ld_tgt = ldx[par].n
            b.wait("vector", ldx[par], ld_tgt)
            b.run("vector", lambda e, p=par: e.reduce_sum(
                cols[:, 0:1], xt[p][:], axis=AX), s_v)
            b.wait("scalar", ldx[par], ld_tgt)
            if tau >= 1:
                b.wait("scalar", s_v, v_apply[tau - 1])
            b.run("scalar", lambda e, p=par: e.activation(
                xsq[:], xt[p][:], AF.Square), s_s)
            sq_tgt = s_s.n
            b.wait("vector", s_s, sq_tgt)
            b.run("vector", lambda e: e.reduce_sum(cols[:, 2:3], xsq[:], axis=AX),
                  s_v)
            b.run("vector", lambda e: e.tensor_scalar(
                cols[:, 1:2], cols[:, 0:1], 1.0 / D, None, ALU.mult), s_v)
            b.run("vector", lambda e: e.tensor_scalar(
                cols[:, 3:4], cols[:, 2:3], 1.0 / D, None, ALU.mult), s_v)
            b.run("vector", lambda e: e.tensor_tensor(
                cols[:, 6:7], cols[:, 1:2], cols[:, 1:2], ALU.mult), s_v)
            b.run("vector", lambda e: e.tensor_scalar(
                cols[:, 6:7], cols[:, 6:7], -1.0, None, ALU.mult), s_v)
            b.run("vector", lambda e: e.tensor_tensor(
                cols[:, 3:4], cols[:, 3:4], cols[:, 6:7], ALU.add), s_v)
            var_tgt = s_v.n
            b.wait("scalar", s_v, var_tgt)
            b.run("scalar", lambda e: e.activation(
                cols[:, 4:5], cols[:, 3:4], AF.Sqrt, bias=epsc[:, 0:1]), s_s)
            std_tgt = s_s.n
            b.wait("vector", s_s, std_tgt)
            b.run("vector", lambda e: e.reciprocal(cols[:, 5:6], cols[:, 4:5]), s_v)
            if tau >= 2:
                b.wait("vector", s_pe, pe_done[tau - 2])
            b.run("vector", lambda e, p=par: e.tensor_scalar(
                xnb[p][:], xt[p][:], cols[:, 1:2], cols[:, 5:6],
                ALU.subtract, ALU.mult), s_v)
            v_apply.append(s_v.n)
            b.wait("tensor", s_v, v_apply[-1])
            for qd in range(8):
                reg = qd % 2
                gq_i = tau * 8 + qd
                if gq_i >= 2:
                    b.wait("tensor", s_s, ev_done[gq_i - 2])
                for j in range(4):
                    f = qd * 4 + j
                    b.run("tensor", lambda e, p=par, ff=f, r=reg, jj=j: e.matmul(
                        ps[:, r * 512 + jj * 128: r * 512 + (jj + 1) * 128],
                        xnb[p][:, ff * 128:(ff + 1) * 128], ident[:],
                        start=(jj == 0), stop=(jj == 3)),
                        s_pe if j == 3 else None)
                b.wait("scalar", s_pe, s_pe.n)
                b.run("scalar", lambda e, r=reg, qd_=qd, t_=tau: e.copy(
                    xnT_v[:, qd_ * 4:(qd_ + 1) * 4, t_ * 128:(t_ + 1) * 128],
                    ps[:, r * 512:(r + 1) * 512].rearrange(
                        "p (a c) -> p a c", a=4)), s_s)
                ev_done.append(s_s.n)
            pe_done.append(s_pe.n)

    with ExitStack() as sA:
        eA = sA.enter_context
        xnT = eA(nc.sbuf_tensor("xnT", [128, DC * T], BF16))

        with ExitStack() as sA1:
            e1 = sA1.enter_context
            sc1 = dict(
                xt=(e1(nc.sbuf_tensor("xt0", [128, D], F32)),
                    e1(nc.sbuf_tensor("xt1", [128, D], F32))),
                xsq=e1(nc.sbuf_tensor("xsq", [128, D], F32)),
                xnb=(e1(nc.sbuf_tensor("xnb0", [128, D], BF16)),
                     e1(nc.sbuf_tensor("xnb1", [128, D], BF16))),
                cols=e1(nc.sbuf_tensor("cols", [128, 8], F32)),
                ps=e1(nc.psum_tensor("ps_a", [128, 1024], F32)))
            ln_transpose(x_in, xnT, sc1)
        b.barrier()

        wsb = (eA(nc.sbuf_tensor("wsb0", [128, DC * 512], BF16)),
               eA(nc.sbuf_tensor("wsb1", [128, DC * 512], BF16)))
        kraw = [eA(nc.sbuf_tensor(f"kraw{i}", [128, 512], F32)) for i in range(4)]
        k2b = [eA(nc.sbuf_tensor(f"k2b{i}", [128, 512], BF16)) for i in range(2)]
        kp2 = [eA(nc.sbuf_tensor(f"kp2{i}", [128, 512], BF16)) for i in range(2)]
        tmpf = [eA(nc.sbuf_tensor(f"tmpf{i}", [128, 512], F32)) for i in range(2)]
        srow = [eA(nc.sbuf_tensor(f"srow{i}", [1, 512], F32)) for i in range(2)]
        rrow = [eA(nc.sbuf_tensor(f"rrow{i}", [1, 512], F32)) for i in range(2)]
        kbf = [eA(nc.sbuf_tensor(f"kbf{i}", [128, 1024], BF16)) for i in range(2)]
        vpc = [eA(nc.sbuf_tensor(f"vpc{i}", [128, 512], BF16)) for i in range(4)]
        ps_p = eA(nc.psum_tensor("ps_p", [128, 1024], F32))
        ps_r1 = eA(nc.psum_tensor("ps_r1", [128, 512], F32))
        ps_bc = eA(nc.psum_tensor("ps_bc", [128, 512], F32))
        ps_r2 = eA(nc.psum_tensor("ps_r2", [128, 512], F32))
        ps_v2 = eA(nc.psum_tensor("ps_v2", [128, 1024], F32))

        xnT_v = xnT[:].rearrange("p (f t) -> p f t", f=DC)

        def emit_proj(w_dram, bias_sb, g_col, out_kind):
            b.wait("tensor", s_s, s_s.n)
            b.wait("tensor", s_v, s_v.n)
            b.wait("scalar", s_v, s_v.n)
            b.wait("vector", s_s, s_s.n)
            b.wait("vector", stk[0], stk[0].n)
            b.wait("vector", stk[1], stk[1].n)

            pe_base, s_base, v_base = s_pe.n, s_s.n, s_v.n
            pe_entry = s_pe.n
            stk_base = [stk[0].n, stk[1].n]
            ldw_base = [ldw[0].n, ldw[1].n]

            def sq_inc(i):
                return s_base + 2 * i + 1
            def srow_inc(i):
                return s_base + 2 * i + 2
            v_kraw = [0] * 64
            v_rr = [0] * 64
            v_tt = [0] * 64
            v_kbf = [0] * 64
            cnt = v_base
            for i in range(64 + 3):
                if i < 64:
                    cnt += 1
                    v_kraw[i] = cnt
                if 0 <= i - 1 < 64:
                    cnt += 1
                    v_rr[i - 1] = cnt
                if 0 <= i - 3 < 64:
                    cnt += 2
                    v_tt[i - 3] = cnt - 1
                    v_kbf[i - 3] = cnt
            pe_proj = [0] * 64
            pe_sumsq = [0] * 64
            pe_bc = [0] * 64

            for i in range(64 + 4):
                if 0 <= i - 3 < 64:
                    j = i - 3
                    b.wait("tensor", s_v, v_rr[j])
                    if j >= 1:
                        b.wait("tensor", s_v, v_tt[j - 1])
                    b.run("tensor", lambda e, jj=j % 2: e.matmul(
                        ps_bc[:], ones_f[0:1, :], rrow[jj][0:1, :],
                        start=True, stop=True), s_pe)
                    pe_bc[j] = s_pe.n
                if i < 64:
                    h, th = divmod(i, 2)
                    p = h // 4
                    oc = h % 4
                    par = i % 2
                    if th == 0 and oc == 0:
                        b.wait("tensor", ldw[p % 2],
                               ldw_base[p % 2] + 16 * (p // 2 + 1))
                    if i >= 2:
                        b.wait("tensor", s_v, v_kraw[i - 2])
                    for f in range(DC):
                        b.run("tensor", lambda e, pp=p, o=oc, tt=th, ff=f, pr=par:
                              e.matmul(ps_p[:, pr * 512:(pr + 1) * 512],
                                       wsb[pp % 2][:, ff * 512 + o * 128:
                                                   ff * 512 + (o + 1) * 128],
                                       xnT_v[:, ff, tt * 512:(tt + 1) * 512],
                                       start=(ff == 0), stop=(ff == DC - 1)),
                              s_pe if f == DC - 1 else None)
                    pe_proj[i] = s_pe.n
                if 0 <= i - 1 < 64:
                    j = i - 1
                    b.wait("tensor", s_s, sq_inc(j))
                    if j >= 2:
                        b.wait("tensor", s_s, srow_inc(j - 2))
                    b.run("tensor", lambda e, jj=j % 2: e.matmul(
                        ps_r1[0:1, :], ones_b[:, 0:1], k2b[jj][:],
                        start=True, stop=True), s_pe)
                    pe_sumsq[j] = s_pe.n

            for i in range(64):
                b.wait("scalar", s_v, v_kraw[i])
                if i >= 2:
                    b.wait("scalar", s_pe, pe_sumsq[i - 2])
                b.run("scalar", lambda e, i4=i % 4, pr=i % 2: e.activation(
                    k2b[pr][:], kraw[i4][:], AF.Square), s_s)
                b.wait("scalar", s_pe, pe_sumsq[i])
                if i >= 2:
                    b.wait("scalar", s_v, v_rr[i - 2])
                b.run("scalar", lambda e, jj=i % 2: e.activation(
                    srow[jj][0:1, :], ps_r1[0:1, :], AF.Sqrt, scale=1.0 / HD,
                    bias=epsc[0:1, 1:2]), s_s)

            for i in range(64 + 3):
                if i < 64:
                    h, th = divmod(i, 2)
                    b.wait("vector", s_pe, pe_proj[i])
                    if i >= 4:
                        b.wait("vector", s_s, sq_inc(i - 4))
                    b.run("vector", lambda e, hh=h, pr=i % 2, i4=i % 4:
                          e.tensor_scalar(
                              kraw[i4][:], ps_p[:, pr * 512:(pr + 1) * 512],
                              bias_sb[:, hh:hh + 1], None, ALU.add), s_v)
                if 0 <= i - 1 < 64:
                    j = i - 1
                    b.wait("vector", s_s, srow_inc(j))
                    if j >= 2:
                        b.wait("vector", s_pe, pe_bc[j - 2])
                    b.run("vector", lambda e, jj=j % 2: e.reciprocal(
                        rrow[jj][0:1, :], srow[jj][0:1, :]), s_v)
                if 0 <= i - 3 < 64:
                    j = i - 3
                    h, th = divmod(j, 2)
                    b.wait("vector", s_pe, pe_bc[j])
                    b.run("vector", lambda e, i4=j % 4, pr=j % 2: e.tensor_tensor(
                        tmpf[pr][:], kraw[i4][:], ps_bc[:], ALU.mult), s_v)
                    if j >= 4:
                        b.wait("vector", stk[h % 2],
                               stk_base[h % 2] + 16 * (h // 2))
                    b.run("vector", lambda e, hh=h, tt=th, pr=j % 2:
                          e.tensor_scalar(
                              kbf[hh % 2][:, tt * 512:(tt + 1) * 512],
                              tmpf[pr][:], g_col[:, 0:1], None, ALU.mult), s_v)

            w_re = w_dram.rearrange("(f p) o -> p f o", p=128)
            for p in range(8):
                if p >= 2:
                    b.wait("gpsimd", s_pe, pe_proj[(p - 2) * 8 + 7])
                else:
                    b.wait("gpsimd", s_pe, pe_entry)
                b.dma("gpsimd", wsb[p % 2][:].rearrange("p (f o) -> p f o", f=DC),
                      w_re[:, :, p * 512:(p + 1) * 512], ldw[p % 2])

            for h in range(H):
                b.wait("sync", s_v, v_kbf[2 * h + 1])
                dst = (cc_in[h * 128:(h + 1) * 128, :] if out_kind == "k"
                       else q_dram[h * 128:(h + 1) * 128, :])
                b.dma("sync", dst, kbf[h % 2][:], stk[h % 2])

        emit_proj(wk, bk_sb, gk_sb, "k")

        b.wait("tensor", s_s, s_s.n)
        b.wait("vector", s_s, s_s.n)
        v_pe_base, v_v_base = s_pe.n, s_v.n
        v_st_base = [s.n for s in stv]
        v_ld_base = [ldw[0].n, ldw[1].n]
        def vg_pe(gi):
            return v_pe_base + gi + 1
        def vg_v(gi, t2):
            return v_v_base + gi * 2 + t2 + 1
        w_re = wv.rearrange("(f p) o -> p f o", p=128)
        for p in range(8):
            if p >= 2:
                b.wait("gpsimd", s_pe, vg_pe((p - 1) * 4 - 1))
            else:
                b.wait("gpsimd", s_pe, v_pe_base)
            b.dma("gpsimd", wsb[p % 2][:].rearrange("p (f o) -> p f o", f=DC),
                  w_re[:, :, p * 512:(p + 1) * 512], ldw[p % 2])
        for p in range(8):
            b.wait("tensor", ldw[p % 2], v_ld_base[p % 2] + 16 * (p // 2 + 1))
            for tcg in range(4):
                gi = p * 4 + tcg
                if gi >= 1:
                    b.wait("tensor", s_v, vg_v(gi - 1, 1))
                for f in range(DC):
                    for t2 in range(2):
                        tc = tcg * 2 + t2
                        b.run("tensor", lambda e, ff=f, t2_=t2, tc_=tc, pp=p:
                              e.matmul(ps_v2[:, t2_ * 512:(t2_ + 1) * 512],
                                       xnT_v[:, ff, tc_ * 128:(tc_ + 1) * 128],
                                       wsb[pp % 2][:, ff * 512:(ff + 1) * 512],
                                       start=(ff == 0), stop=(ff == DC - 1)),
                              s_pe if (f == DC - 1 and t2 == 1) else None)
                pe_tgt = s_pe.n
                for t2 in range(2):
                    tc = tcg * 2 + t2
                    b.wait("vector", s_pe, pe_tgt)
                    vslot = (gi * 2 + t2) % 4
                    if gi * 2 + t2 >= 4:
                        b.wait("vector", stv[vslot],
                               v_st_base[vslot] + 16 * ((gi * 2 + t2) // 4))
                    b.run("vector", lambda e, sl=vslot, t2_=t2: e.tensor_copy(
                        vpc[sl][:], ps_v2[:, t2_ * 512:(t2_ + 1) * 512]), s_v)
                    b.wait("sync", s_v, s_v.n)
                    off = (p // 2) * 1024 + (p % 2) * 512
                    dst = cc_in[D + tc * 512:D + (tc + 1) * 512, :].rearrange(
                        "(t f) c -> t (f c)", t=128)[:, off:off + 512]
                    b.dma("sync", dst, vpc[vslot][:], stv[vslot])

        for s_ in stk + stv:
            b.wait("gpsimd", s_, s_.n)
        b.run("gpsimd", lambda e: e.collective_compute(
            "AllGather", ALU.bypass,
            replica_groups=[[0, 1, 2, 3], [4, 5, 6, 7]],
            ins=[cc_in.opt()], outs=[cc_out.opt()]), s_cc, 1)

        emit_proj(wq, bq_sb, gq_sb, "q")
        q_st_final = [stk[0].n, stk[1].n]
        b.barrier()

    with ExitStack() as sB:
        eB = sB.enter_context
        ao = eB(nc.sbuf_tensor("ao", [128, H * T], BF16))

        with ExitStack() as sB1:
            e4 = sB1.enter_context
            probsT = e4(nc.sbuf_tensor("probsT", [128, DC * 512], BF16))
            khb = [e4(nc.sbuf_tensor(f"khb{i}", [128, 4096], BF16))
                   for i in range(2)]
            vhb = [e4(nc.sbuf_tensor(f"vhb{i}", [128, 4096], BF16))
                   for i in range(2)]
            qhb = [e4(nc.sbuf_tensor(f"qhb{i}", [128, 1024], BF16))
                   for i in range(2)]
            recb = [e4(nc.sbuf_tensor(f"recb{i}", [128, 512], F32))
                    for i in range(2)]
            tta = [e4(nc.sbuf_tensor(f"tta{i}", [128, 512], F32))
                   for i in range(2)]
            grow = e4(nc.sbuf_tensor("grow", [1, 256], F32))
            nbcol = e4(nc.sbuf_tensor("nbcol", [128, 1], F32))
            srow2 = e4(nc.sbuf_tensor("srow2", [1, 512], F32))
            rrow2 = e4(nc.sbuf_tensor("rrow2", [1, 512], F32))
            ps_sc = e4(nc.psum_tensor("ps_sc", [128, 2048], F32))
            ps_pv = e4(nc.psum_tensor("ps_pv", [128, 1024], F32))
            ps_sum = e4(nc.psum_tensor("ps_sum", [128, 512], F32))
            ps_m = e4(nc.psum_tensor("ps_m", [128, 512], F32))

            cc_r = cc_out.rearrange("(r z) c -> r z c", r=4)
            cc_v5 = cc_r[:, D:2 * D, :].rearrange(
                "r (tt t f) c -> r tt t f c", tt=TC, t=128, f=4)

            b.run("tensor", lambda e: e.matmul(
                ps_m[0:1, 0:128], gq_sb[:, 0:1], idf_in_sb[:],
                start=True, stop=True), s_pe)
            b.run("tensor", lambda e: e.matmul(
                ps_m[0:1, 128:256], gk_sb[:, 0:1], idf_in_sb[:],
                start=True, stop=True), s_pe)
            gt_tgt = s_pe.n
            b.wait("scalar", s_pe, gt_tgt)
            b.run("scalar", lambda e: e.copy(grow[0:1, :], ps_m[0:1, 0:256]), s_s)
            g_cp = s_s.n
            b.wait("vector", s_s, g_cp)
            b.run("vector", lambda e: e.tensor_tensor(
                grow[0:1, :], grow[0:1, :], grow[0:1, :], ALU.mult), s_v)
            b.run("vector", lambda e: e.reduce_max(
                grow[0:1, 0:1], grow[0:1, 0:128], axis=AX), s_v)
            b.run("vector", lambda e: e.reduce_max(
                grow[0:1, 1:2], grow[0:1, 128:256], axis=AX), s_v)
            b.run("vector", lambda e: e.tensor_tensor(
                grow[0:1, 0:1], grow[0:1, 0:1], grow[0:1, 1:2], ALU.mult), s_v)
            g_m2 = s_v.n
            b.wait("scalar", s_v, g_m2)
            b.run("scalar", lambda e: e.activation(
                grow[0:1, 2:3], grow[0:1, 0:1], AF.Sqrt), s_s)
            g_sq = s_s.n
            b.wait("vector", s_s, g_sq)
            b.run("vector", lambda e: e.tensor_scalar(
                grow[0:1, 2:3], grow[0:1, 2:3], -128.0, None, ALU.mult), s_v)
            nb_tgt = s_v.n
            b.wait("tensor", s_v, nb_tgt)
            b.run("tensor", lambda e: e.matmul(
                ps_m[:, 0:1], ones_f[0:1, :], grow[0:1, 2:3],
                start=True, stop=True), s_pe)
            bc_all = s_pe.n
            b.wait("scalar", s_pe, bc_all)
            b.run("scalar", lambda e: e.copy(nbcol[:], ps_m[:, 0:1]), s_s)
            nbcol_tgt = s_s.n

            pe_base = s_pe.n
            import os
            if os.environ.get("KDBG"):
                print("P4 pe_base", pe_base, "s_base", s_s.n, "v_base", s_v.n)
            per_nh = 19
            def sc_pair_inc(h, nh, mcg):
                return pe_base + (h * 2 + nh) * per_nh + mcg + 1
            def sums_inc(h, nh):
                return pe_base + (h * 2 + nh) * per_nh + 17
            def pv_inc(h, nh):
                return pe_base + (h * 2 + nh) * per_nh + 18
            def bc2_inc(h, nh):
                return pe_base + (h * 2 + nh) * per_nh + 19
            s_base = s_s.n
            def exp_inc(h, nh, mcg):
                return s_base + (h * 2 + nh) * 18 + mcg + 1
            def srow_cp_inc(h, nh):
                return s_base + (h * 2 + nh) * 18 + 17
            def rec_inc(h, nh):
                return s_base + (h * 2 + nh) * 18 + 18
            v_base = s_v.n
            def rr2_inc(h, nh):
                return v_base + (h * 2 + nh) * 3 + 1
            def aot_inc(h, nh):
                return v_base + (h * 2 + nh) * 3 + 2
            def aow_inc(h, nh):
                return v_base + (h * 2 + nh) * 3 + 3
            ldk_base = [ldk[0].n, ldk[1].n]
            ldv_base = [ldv[0].n, ldv[1].n]
            ldq_base = [ldq[0].n, ldq[1].n]

            b.wait("sync", s_cc, s_cc.n)
            b.wait("sync", stk[0], q_st_final[0])
            b.wait("sync", stk[1], q_st_final[1])
            for h in range(H):
                if h >= 2:
                    b.wait("sync", s_pe, pv_inc(h - 2, 1))
                b.dma("sync", khb[h % 2][:].rearrange("p (r t) -> p r t", r=4),
                      cc_r[:, h * 128:(h + 1) * 128, :].transpose([1, 0, 2]),
                      ldk[h % 2])
                for r in range(4):
                    off = (h // 8) * 1024 + (h % 8) * 128
                    vsrc = cc_out[r * RS + D:r * RS + 2 * D, :].rearrange(
                        "(tt t f) c -> tt t (f c)", tt=TC, t=128)[
                        :, :, off:off + 128].transpose([1, 0, 2])
                    b.dma("sync",
                          vhb[h % 2][:, r * 1024:(r + 1) * 1024].rearrange(
                              "p (tt d) -> p tt d", tt=TC),
                          vsrc, ldv[h % 2])
                b.dma("sync", qhb[h % 2][:], q_dram[h * 128:(h + 1) * 128, :],
                      ldq[h % 2])

            for h in range(H):
                b.wait("tensor", ldk[h % 2], ldk_base[h % 2] + 16 * (h // 2 + 1))
                b.wait("tensor", ldv[h % 2], ldv_base[h % 2] + 64 * (h // 2 + 1))
                b.wait("tensor", ldq[h % 2], ldq_base[h % 2] + 16 * (h // 2 + 1))
                for nh in range(2):
                    for mcg in range(16):
                        reg = mcg % 2
                        gidx = (h * 2 + nh) * 16 + mcg
                        if gidx >= 2:
                            pm = gidx - 2
                            hh, rem = divmod(pm, 32)
                            nn, mm = divmod(rem, 16)
                            b.wait("tensor", s_s, exp_inc(hh, nn, mm))
                        for j in range(2):
                            mc = mcg * 2 + j
                            b.run("tensor",
                                  lambda e, hh=h, n_=nh, m_=mc, r=reg, jj=j:
                                  e.matmul(ps_sc[:, r * 1024 + jj * 512:
                                                 r * 1024 + (jj + 1) * 512],
                                           khb[hh % 2][:, m_ * 128:(m_ + 1) * 128],
                                           qhb[hh % 2][:, n_ * 512:(n_ + 1) * 512],
                                           start=True, stop=True),
                                  s_pe if j == 1 else None)
                    b.wait("tensor", s_s, exp_inc(h, nh, 15))
                    if h * 2 + nh >= 1:
                        pi = h * 2 + nh - 1
                        b.wait("tensor", s_s, srow_cp_inc(pi // 2, pi % 2))
                    for mc in range(DC):
                        b.run("tensor", lambda e, m_=mc: e.matmul(
                            ps_sum[0:1, :], ones_b[:, 0:1],
                            probsT[:, m_ * 512:(m_ + 1) * 512],
                            start=(m_ == 0), stop=(m_ == DC - 1)),
                            s_pe if mc == DC - 1 else None)
                    if h * 2 + nh >= 2:
                        pi = h * 2 + nh - 2
                        b.wait("tensor", s_v, aot_inc(pi // 2, pi % 2))
                    for mc in range(DC):
                        b.run("tensor", lambda e, hh=h, n_=nh, m_=mc: e.matmul(
                            ps_pv[:, (n_ % 2) * 512:(n_ % 2 + 1) * 512],
                            vhb[hh % 2][:, m_ * 128:(m_ + 1) * 128],
                            probsT[:, m_ * 512:(m_ + 1) * 512],
                            start=(m_ == 0), stop=(m_ == DC - 1)),
                            s_pe if mc == DC - 1 else None)
                    b.wait("tensor", s_v, rr2_inc(h, nh))
                    if h * 2 + nh >= 1:
                        pi = h * 2 + nh - 1
                        b.wait("tensor", s_s, rec_inc(pi // 2, pi % 2))
                    else:
                        b.wait("tensor", s_s, nbcol_tgt)
                    b.run("tensor", lambda e: e.matmul(
                        ps_m[:], ones_f[0:1, :], rrow2[0:1, :],
                        start=True, stop=True), s_pe)

                for nh in range(2):
                    for mcg in range(16):
                        b.wait("scalar", s_pe, sc_pair_inc(h, nh, mcg))
                        if nh == 0 and mcg == 0:
                            if h >= 1:
                                b.wait("scalar", s_pe, pv_inc(h - 1, 1))
                            else:
                                b.wait("scalar", s_s, nbcol_tgt)
                        if nh == 1 and mcg == 0:
                            b.wait("scalar", s_pe, pv_inc(h, 0))
                        b.run("scalar", lambda e, m_=mcg: e.activation(
                            probsT[:, m_ * 1024:(m_ + 1) * 1024],
                            ps_sc[:, (m_ % 2) * 1024:(m_ % 2 + 1) * 1024],
                            AF.Exp, bias=nbcol[:, 0:1]), s_s)
                    b.wait("scalar", s_pe, sums_inc(h, nh))
                    if h * 2 + nh >= 1:
                        pi = h * 2 + nh - 1
                        b.wait("scalar", s_v, rr2_inc(pi // 2, pi % 2))
                    b.run("scalar", lambda e: e.copy(srow2[0:1, :],
                                                     ps_sum[0:1, :]), s_s)
                    b.wait("scalar", s_pe, bc2_inc(h, nh))
                    b.run("scalar", lambda e, n_=nh: e.copy(recb[n_][:],
                                                            ps_m[:]), s_s)

                for nh in range(2):
                    b.wait("vector", s_s, srow_cp_inc(h, nh))
                    if h * 2 + nh >= 1:
                        pi = h * 2 + nh - 1
                        b.wait("vector", s_pe, bc2_inc(pi // 2, pi % 2))
                    b.run("vector", lambda e: e.reciprocal(rrow2[0:1, :],
                                                           srow2[0:1, :]), s_v)
                    b.wait("vector", s_pe, pv_inc(h, nh))
                    b.wait("vector", s_s, rec_inc(h, nh))
                    b.run("vector", lambda e, n_=nh: e.tensor_tensor(
                        tta[n_][:], ps_pv[:, (n_ % 2) * 512:(n_ % 2 + 1) * 512],
                        recb[n_][:], ALU.mult), s_v)
                    b.run("vector", lambda e, hh=h, n_=nh: e.tensor_scalar(
                        ao[:, hh * 1024 + n_ * 512:hh * 1024 + (n_ + 1) * 512],
                        tta[n_][:], bv_sb[:, hh:hh + 1], None, ALU.add), s_v)
        b.barrier()

        with ExitStack() as sB2:
            e5 = sB2.enter_context
            wob = [e5(nc.sbuf_tensor(f"wob{i}", [128, DC * 512], BF16))
                   for i in range(2)]
            opan = [e5(nc.sbuf_tensor(f"opan{i}", [128, 4096], BF16))
                    for i in range(2)]
            xsl = [e5(nc.sbuf_tensor(f"xsl{i}", [128, 512], F32))
                   for i in range(4)]
            x1p = [e5(nc.sbuf_tensor(f"x1p{i}", [128, 512], F32))
                   for i in range(4)]
            ps_o = e5(nc.psum_tensor("ps_o", [128, 1024], F32))
            ps_t5 = e5(nc.psum_tensor("ps_t5", [128, 1024], F32))

            pe5, s5, v5 = s_pe.n, s_s.n, s_v.n
            ld5 = [s.n for s in ldx4]
            ldg5 = [ldw[0].n, ldw[1].n]
            st5 = [s.n for s in stx1]
            def o_proj_inc(p, i8):
                return pe5 + p * 16 + i8 + 1
            def o_tq_inc(p, tau):
                return pe5 + p * 16 + 8 + tau + 1
            def o_ev_inc(p, i8):
                return v5 + p * 16 + i8 + 1
            def x1_inc(p, tau):
                return v5 + p * 16 + 8 + tau + 1

            wo_re = wo.rearrange("(f p) o -> p f o", p=128)
            for p in range(8):
                if p >= 2:
                    b.wait("gpsimd", s_pe, o_proj_inc(p - 2, 7))
                b.dma("gpsimd", wob[p % 2][:].rearrange("p (f o) -> p f o", f=DC),
                      wo_re[:, :, p * 512:(p + 1) * 512], ldw[p % 2])
            for p in range(8):
                for tau in range(8):
                    gi = p * 8 + tau
                    if gi >= 4:
                        pj = gi - 4
                        b.wait("sync", s_v, x1_inc(pj // 8, pj % 8))
                    b.dma("sync", xsl[gi % 4][:],
                          x_in[tau * 128:(tau + 1) * 128,
                               p * 512:(p + 1) * 512], ldx4[gi % 4])
            for p in range(8):
                b.wait("tensor", ldw[p % 2], ldg5[p % 2] + 16 * (p // 2 + 1))
                for i8 in range(8):
                    oc, th = divmod(i8, 2)
                    gi8 = p * 8 + i8
                    if gi8 >= 2:
                        pj = gi8 - 2
                        b.wait("tensor", s_v, o_ev_inc(pj // 8, pj % 8))
                    for hc in range(DC):
                        b.run("tensor", lambda e, pp=p, o=oc, tt=th, c=hc, r=i8 % 2:
                              e.matmul(ps_o[:, r * 512:(r + 1) * 512],
                                       wob[pp % 2][:, c * 512 + o * 128:
                                                   c * 512 + (o + 1) * 128],
                                       ao[:, c * 1024 + tt * 512:
                                          c * 1024 + (tt + 1) * 512],
                                       start=(c == 0), stop=(c == DC - 1)),
                              s_pe if hc == DC - 1 else None)
                b.wait("tensor", s_v, o_ev_inc(p, 7))
                for tau in range(8):
                    gq_i = p * 8 + tau
                    if gq_i >= 2:
                        pj = gq_i - 2
                        b.wait("tensor", s_v, x1_inc(pj // 8, pj % 8))
                    for oc in range(4):
                        b.run("tensor", lambda e, pp=p, t_=tau, o=oc: e.matmul(
                            ps_t5[:, (t_ % 2) * 512 + o * 128:
                                  (t_ % 2) * 512 + (o + 1) * 128],
                            opan[pp % 2][:, o * 1024 + t_ * 128:
                                         o * 1024 + (t_ + 1) * 128],
                            ident[:], start=(o == 0), stop=(o == 3)),
                            s_pe if oc == 3 else None)
            for p in range(8):
                for i8 in range(8):
                    oc, th = divmod(i8, 2)
                    b.wait("vector", s_pe, o_proj_inc(p, i8))
                    if p >= 2 and i8 == 0:
                        b.wait("vector", s_pe, o_tq_inc(p - 2, 7))
                    b.run("vector", lambda e, pp=p, o=oc, tt=th, r=i8 % 2:
                          e.tensor_scalar(
                              opan[pp % 2][:, o * 1024 + tt * 512:
                                           o * 1024 + (tt + 1) * 512],
                              ps_o[:, r * 512:(r + 1) * 512],
                              bo_sb[:, pp * 4 + o:pp * 4 + o + 1], None,
                              ALU.add), s_v)
                for tau in range(8):
                    gi = p * 8 + tau
                    b.wait("vector", s_pe, o_tq_inc(p, tau))
                    b.wait("vector", ldx4[gi % 4], ld5[gi % 4] + 16 * (gi // 4 + 1))
                    if gi >= 4:
                        b.wait("vector", stx1[gi % 4], st5[gi % 4] + 16 * (gi // 4))
                    b.run("vector", lambda e, t_=tau, g=gi: e.tensor_tensor(
                        x1p[g % 4][:],
                        ps_t5[:, (t_ % 2) * 512:((t_ % 2) + 1) * 512],
                        xsl[g % 4][:], ALU.add), s_v)
            for p in range(8):
                for tau in range(8):
                    gi = p * 8 + tau
                    b.wait("scalar", s_v, x1_inc(p, tau))
                    b.dma("scalar",
                          x1_dram[tau * 128:(tau + 1) * 128,
                                  p * 512:(p + 1) * 512],
                          x1p[gi % 4][:], stx1[gi % 4])
        b.barrier()
    x1_st_final = [s.n for s in stx1]

    with ExitStack() as sD:
        eD = sD.enter_context
        xn2T = eD(nc.sbuf_tensor("xn2T", [128, DC * T], BF16))

        with ExitStack() as sD1:
            e6 = sD1.enter_context
            sc2 = dict(
                xt=(e6(nc.sbuf_tensor("xt0b", [128, D], F32)),
                    e6(nc.sbuf_tensor("xt1b", [128, D], F32))),
                xsq=e6(nc.sbuf_tensor("xsqb", [128, D], F32)),
                xnb=(e6(nc.sbuf_tensor("xnb0b", [128, D], BF16)),
                     e6(nc.sbuf_tensor("xnb1b", [128, D], BF16))),
                cols=e6(nc.sbuf_tensor("colsb", [128, 8], F32)),
                ps=e6(nc.psum_tensor("ps_a2", [128, 1024], F32)))
            ln_transpose(x1_dram, xn2T, sc2,
                         pre_wait=list(zip(stx1, x1_st_final)))
        b.barrier()

        sgz = {}
        for nm, src in (("sg", sg_in), ("zg", zg_in), ("su", su_in),
                        ("zu", zu_in)):
            sgz[nm] = eD(nc.sbuf_tensor(nm + "_sb", [128, ICN * 32], F32))
            b.dma("sync", sgz[nm][:], src, ld_sy)
        b.wait("vector", ld_sy)
        for zn, sn in (("zg", "sg"), ("zu", "su")):
            b.run("vector", lambda e, z=sgz[zn], s=sgz[sn]: e.tensor_tensor(
                z[:], z[:], s[:], ALU.mult), s_v)
            b.run("vector", lambda e, z=sgz[zn]: e.tensor_scalar(
                z[:], z[:], -1.0, None, ALU.mult), s_v)
        nz_tgt = s_v.n

        qg_sb = [eD(nc.sbuf_tensor(f"qg{i}", [128, D], I32)) for i in range(2)]
        qu_sb = [eD(nc.sbuf_tensor(f"qu{i}", [128, D], I32)) for i in range(2)]
        dqg = [eD(nc.sbuf_tensor(f"dqg{i}", [128, 128], BF16)) for i in range(8)]
        dqu = [eD(nc.sbuf_tensor(f"dqu{i}", [128, 128], BF16)) for i in range(8)]
        wgT = eD(nc.sbuf_tensor("wgT", [128, D], BF16))
        wuT = eD(nc.sbuf_tensor("wuT", [128, D], BF16))
        gsb = [eD(nc.sbuf_tensor(f"gsb{i}", [128, 512], BF16)) for i in range(2)]
        usb = [eD(nc.sbuf_tensor(f"usb{i}", [128, 512], BF16)) for i in range(2)]
        hac = [eD(nc.sbuf_tensor(f"hac{i}", [128, 1024], BF16)) for i in range(2)]
        ps_wt = eD(nc.psum_tensor("ps_wt", [128, 1024], F32))
        ps_g = eD(nc.psum_tensor("ps_g", [128, 1024], F32))
        ps_u = eD(nc.psum_tensor("ps_u", [128, 1024], F32))

        xn2T_v = xn2T[:].rearrange("p (f t) -> p f t", f=DC)
        pe6 = s_pe.n
        def wgq_inc(ic, qd):
            return pe6 + ic * 20 + qd + 1
        def wuq_inc(ic, qd):
            return pe6 + ic * 20 + 8 + qd + 1
        def g_inc(ic, th):
            return pe6 + ic * 20 + 16 + th + 1
        def u_inc(ic, th):
            return pe6 + ic * 20 + 18 + th + 1
        s6 = s_s.n
        def wge_inc(ic, qd):
            return s6 + ic * 20 + qd + 1
        def wue_inc(ic, qd):
            return s6 + ic * 20 + 8 + qd + 1
        def silu_inc(ic, th):
            return s6 + ic * 20 + 16 + 2 * th + 1
        def uev_inc(ic, th):
            return s6 + ic * 20 + 16 + 2 * th + 2
        v6 = s_v.n
        import os
        if os.environ.get("KDBG"):
            print("P6 pe6", pe6, "s6", s6, "v6", v6, "g6_pre", s_gp.n)
        def vdq_inc(ic, hc):
            if ic == 0:
                return v6 + hc + 1
            start = v6 + 32 + 34 * (ic - 1)
            return start + hc + 1 + (2 if hc >= 16 else 0)
        def hm_inc(ic, th):
            return v6 + 32 + 34 * ic + 16 + th + 1
        g6 = s_gp.n
        def pdq_inc(ic, hc):
            return g6 + ic * 32 + hc + 1
        ld6g = [ldqg[0].n, ldqg[1].n]
        ld6u = [ldqu[0].n, ldqu[1].n]
        st6 = [sth[0].n, sth[1].n]

        for ic in range(ICN):
            if ic >= 2:
                b.wait("sync", s_v, vdq_inc(ic - 2, 31))
            b.dma("sync", qg_sb[ic % 2][:], qwg[ic * 128:(ic + 1) * 128, :],
                  ldqg[ic % 2])
            if ic >= 2:
                b.wait("sync", s_gp, pdq_inc(ic - 2, 31))
            b.dma("sync", qu_sb[ic % 2][:], qwu[ic * 128:(ic + 1) * 128, :],
                  ldqu[ic % 2])

            b.wait("vector", ldqg[ic % 2], ld6g[ic % 2] + 16 * (ic // 2 + 1))
            if ic == 0:
                b.wait("vector", s_v, nz_tgt)
            for hc in range(DC):
                if hc == 16 and ic >= 1:
                    for th in range(2):
                        b.wait("vector", s_s, uev_inc(ic - 1, th))
                        if ic >= 3:
                            par_h = (ic - 1) % 2
                            b.wait("vector", sth[par_h],
                                   st6[par_h] + 16 * ((ic - 1) // 2))
                        b.run("vector", lambda e, i=ic - 1, tt=th:
                              e.tensor_tensor(
                                  hac[i % 2][:, tt * 512:(tt + 1) * 512],
                                  gsb[tt][:], usb[tt][:], ALU.mult), s_v)
                gt = ic * 32 + hc
                if gt >= 8:
                    gj = gt - 8
                    b.wait("vector", s_pe, wgq_inc(gj // 32, (gj % 32) // 4))
                b.run("vector", lambda e, i=ic, c=hc: e.tensor_scalar(
                    dqg[c % 8][:], qg_sb[i % 2][:, c * 128:(c + 1) * 128],
                    sgz["sg"][:, i * 32 + c:i * 32 + c + 1],
                    sgz["zg"][:, i * 32 + c:i * 32 + c + 1],
                    ALU.mult, ALU.add), s_v)
            b.wait("gpsimd", ldqu[ic % 2], ld6u[ic % 2] + 16 * (ic // 2 + 1))
            if ic == 0:
                b.wait("gpsimd", s_v, nz_tgt)
            for hc in range(DC):
                gt = ic * 32 + hc
                if gt >= 8:
                    gj = gt - 8
                    b.wait("gpsimd", s_pe, wuq_inc(gj // 32, (gj % 32) // 4))
                b.run("gpsimd", lambda e, i=ic, c=hc: e.tensor_scalar(
                    dqu[c % 8][:], qu_sb[i % 2][:, c * 128:(c + 1) * 128],
                    sgz["su"][:, i * 32 + c:i * 32 + c + 1],
                    sgz["zu"][:, i * 32 + c:i * 32 + c + 1],
                    ALU.mult, ALU.add), s_gp)
            for qd in range(8):
                if ic >= 1 or qd >= 2:
                    if qd >= 2:
                        b.wait("tensor", s_s, wge_inc(ic, qd - 2))
                    else:
                        b.wait("tensor", s_s, wue_inc(ic - 1, 6 + qd))
                for j in range(4):
                    hc = qd * 4 + j
                    b.wait("tensor", s_v, vdq_inc(ic, hc))
                    b.run("tensor", lambda e, c=hc, jj=j, q_=qd: e.matmul(
                        ps_wt[:, (q_ % 2) * 512 + jj * 128:
                              (q_ % 2) * 512 + (jj + 1) * 128],
                        dqg[c % 8][:], ident[:], start=(jj == 0), stop=(jj == 3)),
                        s_pe if j == 3 else None)
            for qd in range(8):
                if qd >= 2:
                    b.wait("tensor", s_s, wue_inc(ic, qd - 2))
                else:
                    b.wait("tensor", s_s, wge_inc(ic, 6 + qd))
                for j in range(4):
                    hc = qd * 4 + j
                    b.wait("tensor", s_gp, pdq_inc(ic, hc))
                    b.run("tensor", lambda e, c=hc, jj=j, q_=qd: e.matmul(
                        ps_wt[:, (q_ % 2) * 512 + jj * 128:
                              (q_ % 2) * 512 + (jj + 1) * 128],
                        dqu[c % 8][:], ident[:], start=(jj == 0), stop=(jj == 3)),
                        s_pe if j == 3 else None)
            b.wait("tensor", s_s, wge_inc(ic, 7))
            for th in range(2):
                if ic >= 1:
                    b.wait("tensor", s_s, silu_inc(ic - 1, th))
                for hc in range(DC):
                    b.run("tensor", lambda e, c=hc, tt=th: e.matmul(
                        ps_g[:, tt * 512:(tt + 1) * 512],
                        wgT[:, c * 128:(c + 1) * 128],
                        xn2T_v[:, c, tt * 512:(tt + 1) * 512],
                        start=(c == 0), stop=(c == DC - 1)),
                        s_pe if hc == DC - 1 else None)
            b.wait("tensor", s_s, wue_inc(ic, 7))
            for th in range(2):
                if ic >= 1:
                    b.wait("tensor", s_s, uev_inc(ic - 1, th))
                for hc in range(DC):
                    b.run("tensor", lambda e, c=hc, tt=th: e.matmul(
                        ps_u[:, tt * 512:(tt + 1) * 512],
                        wuT[:, c * 128:(c + 1) * 128],
                        xn2T_v[:, c, tt * 512:(tt + 1) * 512],
                        start=(c == 0), stop=(c == DC - 1)),
                        s_pe if hc == DC - 1 else None)
            for qd in range(8):
                b.wait("scalar", s_pe, wgq_inc(ic, qd))
                b.run("scalar", lambda e, q_=qd: e.copy(
                    wgT[:, q_ * 512:(q_ + 1) * 512],
                    ps_wt[:, (q_ % 2) * 512:((q_ % 2) + 1) * 512]), s_s)
            for qd in range(8):
                b.wait("scalar", s_pe, wuq_inc(ic, qd))
                b.run("scalar", lambda e, q_=qd: e.copy(
                    wuT[:, q_ * 512:(q_ + 1) * 512],
                    ps_wt[:, (q_ % 2) * 512:((q_ % 2) + 1) * 512]), s_s)
            for th in range(2):
                b.wait("scalar", s_pe, g_inc(ic, th))
                if ic >= 1:
                    b.wait("scalar", s_v, hm_inc(ic - 1, th))
                b.run("scalar", lambda e, tt=th: e.activation(
                    gsb[tt][:], ps_g[:, tt * 512:(tt + 1) * 512], AF.Silu), s_s)
                b.wait("scalar", s_pe, u_inc(ic, th))
                b.run("scalar", lambda e, tt=th: e.activation(
                    usb[tt][:], ps_u[:, tt * 512:(tt + 1) * 512], AF.Copy), s_s)
            if ic >= 1:
                b.wait("scalar", s_v, hm_inc(ic - 1, 1))
                b.dma("scalar", hact_dram[(ic - 1) * 128:ic * 128, :],
                      hac[(ic - 1) % 2][:], sth[(ic - 1) % 2])
        par_t = (ICN - 1) % 2
        b.wait("vector", sth[par_t], st6[par_t] + 16 * ((ICN - 1 - par_t) // 2))
        for th in range(2):
            b.wait("vector", s_s, uev_inc(ICN - 1, th))
            b.run("vector", lambda e, i=ICN - 1, tt=th: e.tensor_tensor(
                hac[i % 2][:, tt * 512:(tt + 1) * 512], gsb[tt][:],
                usb[tt][:], ALU.mult), s_v)
        b.wait("scalar", s_v, s_v.n)
        b.dma("scalar", hact_dram[(ICN - 1) * 128:ICN * 128, :],
              hac[(ICN - 1) % 2][:], sth[(ICN - 1) % 2])
        b.barrier()

    with ExitStack() as sF:
        eF = sF.enter_context
        szd = {}
        for nm, src in (("sd", sd_in), ("zd", zd_in)):
            szd[nm] = eF(nc.sbuf_tensor(nm + "_sb", [128, 32 * ICN], F32))
            b.dma("sync", szd[nm][:], src, ld_sy)
        b.wait("vector", ld_sy)
        b.run("vector", lambda e: e.tensor_tensor(
            szd["zd"][:], szd["zd"][:], szd["sd"][:], ALU.mult), s_v)
        b.run("vector", lambda e: e.tensor_scalar(
            szd["zd"][:], szd["zd"][:], -1.0, None, ALU.mult), s_v)
        nzd_tgt = s_v.n

        qd_sb = [eF(nc.sbuf_tensor(f"qd{i}", [128, II], I32)) for i in range(2)]
        dqd = [eF(nc.sbuf_tensor(f"dqd{i}", [128, 128], BF16)) for i in range(4)]
        wdc = [eF(nc.sbuf_tensor(f"wdc{i}", [128, II], BF16)) for i in range(2)]
        ps_w7 = eF(nc.psum_tensor("ps_w7", [128, 1024], F32))

        groups = [list(range(g * 4, min(g * 4 + 4, ICN))) for g in range(22)]
        pe7 = s_pe.n
        def w7q_inc(hc, g):
            return pe7 + hc * 22 + g + 1
        s7 = s_s.n
        def w7e_inc(hc, g):
            return s7 + hc * 22 + g + 1
        v7 = s_v.n
        g7 = s_gp.n
        def dq7_sem_val(hc, ic):
            if ic % 2 == 0:
                return s_v, v7 + hc * 43 + ic // 2 + 1
            return s_gp, g7 + hc * 43 + (ic - 1) // 2 + 1
        ld7 = [ldqd[0].n, ldqd[1].n]
        st7 = [stw[0].n, stw[1].n]

        for hc in range(32):
            if hc >= 2:
                b.wait("sync", s_v, v7 + (hc - 1) * 43)
                b.wait("sync", s_gp, g7 + (hc - 1) * 43)
            b.dma("sync", qd_sb[hc % 2][:], qwd[hc * 128:(hc + 1) * 128, :],
                  ldqd[hc % 2])
            if hc >= 1:
                b.wait("scalar", s_s, w7e_inc(hc - 1, 21))
                b.dma("scalar", wdt_dram[hc - 1, :, :], wdc[(hc - 1) % 2][:],
                      stw[(hc - 1) % 2])
            for eng, par in (("vector", 0), ("gpsimd", 1)):
                b.wait(eng, ldqd[hc % 2], ld7[hc % 2] + 16 * (hc // 2 + 1))
                if hc == 0:
                    b.wait(eng, s_v, nzd_tgt)
                sem = s_v if eng == "vector" else s_gp
                for ic in range(par, ICN, 2):
                    if ic >= 4:
                        b.wait(eng, s_pe, w7q_inc(hc, (ic - 4) // 4))
                    elif hc >= 1:
                        b.wait(eng, s_pe, w7q_inc(hc - 1, 21))
                    b.run(eng, lambda e, h_=hc, i=ic: e.tensor_scalar(
                        dqd[i % 4][:], qd_sb[h_ % 2][:, i * 128:(i + 1) * 128],
                        szd["sd"][:, h_ * ICN + i:h_ * ICN + i + 1],
                        szd["zd"][:, h_ * ICN + i:h_ * ICN + i + 1],
                        ALU.mult, ALU.add), sem)
            for g, ics in enumerate(groups):
                gq_i = hc * 22 + g
                if gq_i >= 2:
                    pj = gq_i - 2
                    b.wait("tensor", s_s, w7e_inc(pj // 22, pj % 22))
                for j, ic in enumerate(ics):
                    dsem, dval = dq7_sem_val(hc, ic)
                    b.wait("tensor", dsem, dval)
                    b.run("tensor", lambda e, i=ic, jj=j, g_=g, n_=len(ics):
                          e.matmul(ps_w7[:, (g_ % 2) * 512 + jj * 128:
                                         (g_ % 2) * 512 + (jj + 1) * 128],
                                   dqd[i % 4][:], ident[:],
                                   start=(jj == 0), stop=(jj == n_ - 1)),
                          s_pe if j == len(ics) - 1 else None)
                b.wait("scalar", s_pe, w7q_inc(hc, g))
                if g == 0 and hc >= 2:
                    b.wait("scalar", stw[hc % 2], st7[hc % 2] + 16 * (hc // 2))
                n_in_g = len(ics)
                b.run("scalar", lambda e, h_=hc, g_=g, n_=n_in_g: e.copy(
                    wdc[h_ % 2][:, g_ * 512:g_ * 512 + n_ * 128],
                    ps_w7[:, (g_ % 2) * 512:(g_ % 2) * 512 + n_ * 128]), s_s)
        b.wait("scalar", s_s, s_s.n)
        b.dma("scalar", wdt_dram[31, :, :], wdc[1][:], stw[1])
        b.barrier()
    wdt_st_final = [stw[0].n, stw[1].n]

    with ExitStack() as sG:
        eG = sG.enter_context
        wdts = [eG(nc.sbuf_tensor(f"wdts{i}", [128, 8 * 512], BF16))
                for i in range(2)]
        hsl = [eG(nc.sbuf_tensor(f"hsl{i}", [128, 4 * 512], BF16))
               for i in range(2)]
        dnsb = eG(nc.sbuf_tensor("dnsb", [128, 8 * 512], BF16))
        x1s = [eG(nc.sbuf_tensor(f"x1s{i}", [128, 1024], F32)) for i in range(4)]
        ypc = [eG(nc.sbuf_tensor(f"ypc{i}", [128, 1024], F32)) for i in range(4)]
        ps_d = eG(nc.psum_tensor("ps_d", [128, 8 * 512], F32))

        hact_r = hact_dram.rearrange("(ic p) t -> ic p t", p=128)
        icbs = [list(range(g * 4, min(g * 4 + 4, ICN))) for g in range(22)]
        pe8 = s_pe.n
        def b_inc(pss, bi):
            return pe8 + pss * 30 + bi + 1
        def tq8_inc(pss, tau, half):
            return pe8 + pss * 30 + 22 + tau * 2 + half + 1
        s8 = s_s.n
        def dne_inc(pss, hc8):
            return s8 + pss * 8 + hc8 + 1
        v8 = s_v.n
        def y_inc(pss, tau):
            return v8 + pss * 8 + 2 * tau + 2
        ld8b = [ldb[0].n, ldb[1].n]
        ld8x = ldx1b.n

        b.wait("sync", stw[0], wdt_st_final[0])
        b.wait("sync", stw[1], wdt_st_final[1])
        b.wait("sync", sth[0], sth[0].n)
        b.wait("sync", sth[1], sth[1].n)
        for s_, v_ in zip(stx1, x1_st_final):
            b.wait("sync", s_, v_)
        for pss in range(8):
            th, grp = divmod(pss, 4)
            for bi, ics in enumerate(icbs):
                gi = pss * 22 + bi
                if gi >= 2:
                    pj = gi - 2
                    b.wait("sync", s_pe, b_inc(pj // 22, pj % 22))
                w = len(ics)
                b.dma("sync",
                      wdts[bi % 2][:, 0:8 * w * 128].rearrange(
                          "p (c i) -> p c i", c=8),
                      wdt_dram[grp * 8:(grp + 1) * 8, :,
                               ics[0] * 128:ics[0] * 128 + w * 128]
                      .transpose([1, 0, 2]), ldb[bi % 2])
                b.dma("sync",
                      hsl[bi % 2][:, 0:w * 512].rearrange(
                          "p (i t) -> p i t", i=w),
                      hact_r[ics[0]:ics[0] + w, :, th * 512:(th + 1) * 512]
                      .transpose([1, 0, 2]), ldb[bi % 2])
            for tau in range(4):
                if pss >= 1:
                    b.wait("sync", s_v, y_inc(pss - 1, tau))
                b.dma("sync", x1s[tau][:],
                      x1_dram[th * 512 + tau * 128:th * 512 + (tau + 1) * 128,
                              grp * 1024:(grp + 1) * 1024], ldx1b)
            for bi, ics in enumerate(icbs):
                gb = pss * 22 + bi
                b.wait("tensor", ldb[bi % 2], ld8b[bi % 2] + 32 * (gb // 2 + 1))
                if bi == 0 and pss >= 1:
                    b.wait("tensor", s_v, y_inc(pss - 1, 3))
                w = len(ics)
                for j, ic in enumerate(ics):
                    for hc8 in range(8):
                        b.run("tensor", lambda e, b_=bi, jj=j, c8=hc8, i=ic,
                              w_=w:
                              e.matmul(ps_d[:, c8 * 512:(c8 + 1) * 512],
                                       wdts[b_ % 2][:, c8 * w_ * 128 + jj * 128:
                                                    c8 * w_ * 128 + (jj + 1) * 128],
                                       hsl[b_ % 2][:, jj * 512:(jj + 1) * 512],
                                       start=(i == 0), stop=(i == ICN - 1)),
                              s_pe if (j == w - 1 and hc8 == 7) else None)
            for hc8 in range(8):
                b.wait("scalar", s_pe, b_inc(pss, 21))
                b.run("scalar", lambda e, c8=hc8: e.copy(
                    dnsb[:, c8 * 512:(c8 + 1) * 512],
                    ps_d[:, c8 * 512:(c8 + 1) * 512]), s_s)
            for tau in range(4):
                for half in range(2):
                    b.wait("tensor", s_s, dne_inc(pss, 7))
                    if tau * 2 + half >= 2:
                        b.wait("tensor", s_v, y_inc(pss, tau - 1))
                    for j in range(4):
                        hc8 = half * 4 + j
                        b.run("tensor", lambda e, c8=hc8, jj=j, hf=half, t_=tau:
                              e.matmul(ps_d[:, hf * 512 + jj * 128:
                                            hf * 512 + (jj + 1) * 128],
                                       dnsb[:, c8 * 512 + t_ * 128:
                                            c8 * 512 + (t_ + 1) * 128],
                                       ident[:], start=(jj == 0), stop=(jj == 3)),
                              s_pe if j == 3 else None)
                b.wait("vector", s_pe, tq8_inc(pss, tau, 1))
                b.wait("vector", ldx1b, ld8x + 64 * (pss + 1))
                if pss >= 1:
                    b.wait("vector", sty4[tau], 16 * pss)
                for half in range(2):
                    b.run("vector", lambda e, t_=tau, hf=half: e.tensor_tensor(
                        ypc[t_][:, hf * 512:(hf + 1) * 512],
                        ps_d[:, hf * 512:(hf + 1) * 512],
                        x1s[t_][:, hf * 512:(hf + 1) * 512], ALU.add), s_v)
                b.wait("scalar", s_v, y_inc(pss, tau))
                b.dma("scalar",
                      y_out[th * 512 + tau * 128:th * 512 + (tau + 1) * 128,
                            grp * 1024:(grp + 1) * 1024],
                      ypc[tau][:], sty4[tau])
        for s_ in sty4:
            b.wait("sync", s_, s_.n)
        b.barrier(drain=False)

    b.finalize_block()
    b.ctx.close()
    return nc


_NC_CACHE = None


def kernel(**inputs):
    global _NC_CACHE
    xs = np.ascontiguousarray(np.asarray(inputs["L0_attn_hidden_states"]))
    ident = np.eye(128, dtype=ml_dtypes.bfloat16)
    ones_b = np.ones((128, 128), dtype=ml_dtypes.bfloat16)
    ones_f = np.ones((128, 128), dtype=np.float32)

    def pcol(v):
        return np.ascontiguousarray(np.asarray(v, np.float32).reshape(32, 128).T)

    def prow_gu(v):
        return np.ascontiguousarray(
            np.asarray(v, np.float32).reshape(ICN, 128, 32)
            .transpose(1, 0, 2).reshape(128, -1))

    def prow_d(v):
        return np.ascontiguousarray(
            np.asarray(v, np.float32).reshape(32, 128, ICN)
            .transpose(1, 0, 2).reshape(128, -1))

    common = dict(
        wq=np.ascontiguousarray(np.asarray(inputs["L0_attn_Wq"], np.float32)),
        wk=np.ascontiguousarray(np.asarray(inputs["L0_attn_Wk"], np.float32)),
        wv=np.ascontiguousarray(np.asarray(inputs["L0_attn_Wv"], np.float32)),
        wo=np.ascontiguousarray(np.asarray(inputs["L0_attn_Wo"], np.float32)),
        qwg=np.ascontiguousarray(np.asarray(inputs["L0_ffn_qw_gate"], np.int32)),
        qwu=np.ascontiguousarray(np.asarray(inputs["L0_ffn_qw_up"], np.int32)),
        qwd=np.ascontiguousarray(np.asarray(inputs["L0_ffn_qw_down"], np.int32)),
        sg=prow_gu(inputs["L0_ffn_s_gate"]), zg=prow_gu(inputs["L0_ffn_z_gate"]),
        su=prow_gu(inputs["L0_ffn_s_up"]), zu=prow_gu(inputs["L0_ffn_z_up"]),
        sd=prow_d(inputs["L0_ffn_s_down"]), zd=prow_d(inputs["L0_ffn_z_down"]),
        bq_p=pcol(inputs["L0_attn_bq"]), bk_p=pcol(inputs["L0_attn_bk"]),
        bv_p=pcol(inputs["L0_attn_bv"]), bo_p=pcol(inputs["L0_attn_bo"]),
        gq_c=np.ascontiguousarray(
            np.asarray(inputs["L0_attn_gq"], np.float32).reshape(128, 1)),
        gk_c=np.ascontiguousarray(
            np.asarray(inputs["L0_attn_gk"], np.float32).reshape(128, 1)),
        ident_bf=ident, ones_bf=ones_b, ones_f=ones_f,
        ident_f=np.eye(128, dtype=np.float32),
    )
    in_maps = []
    for c in range(8):
        bb, r = divmod(c, 4)
        m = dict(common)
        m["x_in"] = np.ascontiguousarray(xs[bb, r * T:(r + 1) * T, :], np.float32)
        in_maps.append(m)

    if _NC_CACHE is None:
        _NC_CACHE = build_program()
    res = run_bass_kernel_spmd(_NC_CACHE, in_maps, core_ids=list(range(8)))
    out = np.empty((B, N, D), np.float32)
    for c in range(8):
        bb, r = divmod(c, 4)
        out[bb, r * T:(r + 1) * T, :] = res.results[c]["y"]
    return out


# version 04
# speedup vs baseline: 1.1587x; 1.1587x over previous
import math
from contextlib import ExitStack

import numpy as np
import ml_dtypes

import concourse.bass as bass
import concourse.mybir as mybir
from concourse.bass_utils import run_bass_kernel_spmd

F32 = mybir.dt.float32
BF16 = mybir.dt.bfloat16
I32 = mybir.dt.int32
AF = mybir.ActivationFunctionType
ALU = mybir.AluOpType
AX = mybir.AxisListType.X

B, N, D = 2, 4096, 4096
H, HD = 32, 128
II = 11008
T = 1024
TC = T // 128
DC = D // 128
ICN = II // 128
RS = 2 * D
INV_SQRT_HD = 1.0 / math.sqrt(HD)
ENGS = ("sync", "tensor", "vector", "scalar", "gpsimd")
LN_EPS = 1e-5
RMS_EPS = 1e-6


class Sem:
    __slots__ = ("h", "n")

    def __init__(self, h):
        self.h = h
        self.n = 0


class Builder:
    def __init__(self):
        self.nc = bass.Bass("TRN2", target_bir_lowering=False)
        self.q = {e: [] for e in ENGS}
        self.ctx = ExitStack()
        self.bar_k = 0

    def sem(self, name):
        return Sem(self.ctx.enter_context(self.nc.semaphore(name)))

    def semlist(self, name, k):
        return [self.sem(f"{name}{i}") for i in range(k)]

    def sbuf(self, name, shape, dtype):
        return self.ctx.enter_context(self.nc.sbuf_tensor(name, shape, dtype))

    def emit(self, eng, fn):
        self.q[eng].append(fn)

    def wait(self, eng, sem, val=None):
        v = sem.n if val is None else val
        if v > 0:
            self.emit(eng, lambda e, s=sem.h, vv=v: e.wait_ge(s, vv))

    SELF_SYNC = ("vector", "scalar", "gpsimd")

    def run(self, eng, build, sem=None, k=1):
        if eng in self.SELF_SYNC and sem is not None and sem.n > 0:
            self.wait(eng, sem, sem.n)
        if sem is not None:
            sem.n += k
        def f(e, bld=build, s=sem, kk=k):
            ins = bld(e)
            if s is not None:
                ins.then_inc(s.h, kk)
        self.emit(eng, f)
        return sem.n if sem is not None else None

    def dma(self, eng, out, in_, sem, **kw):
        return self.run(eng, lambda e, o=out, i=in_, k=kw: e.dma_start(out=o, in_=i, **k),
                        sem, 16)

    def barrier(self, drain=True):
        self.bar_k += 1
        tgt = 5 * self.bar_k
        for eng in ENGS:
            if drain and eng in ("tensor", "vector", "scalar", "gpsimd"):
                self.emit(eng, lambda e: e.drain())
            self.emit(eng, lambda e, s=self.bar.h: e.sem_inc(s, 1))
            self.emit(eng, lambda e, s=self.bar.h, v=tgt: e.wait_ge(s, v))

    def finalize_block(self):
        nc = self.nc
        with nc.Block() as blk:
            for eng in ENGS:
                fns = self.q[eng]
                getattr(blk, eng)(lambda e, fns=fns: [fn(e) for fn in fns])


def build_program():
    b = Builder()
    nc = b.nc

    def din(name, shape, dtype):
        return nc.dram_tensor(name, shape, dtype, kind="ExternalInput").ap()

    x_in = din("x_in", [T, D], F32)
    wq = din("wq", [D, D], F32)
    wk = din("wk", [D, D], F32)
    wv = din("wv", [D, D], F32)
    wo = din("wo", [D, D], F32)
    qwg = din("qwg", [II, D], I32)
    qwu = din("qwu", [II, D], I32)
    qwd = din("qwd", [D, II], I32)
    sg_in = din("sg", [128, ICN * 32], F32)
    zg_in = din("zg", [128, ICN * 32], F32)
    su_in = din("su", [128, ICN * 32], F32)
    zu_in = din("zu", [128, ICN * 32], F32)
    sd_in = din("sd", [128, 32 * ICN], F32)
    zd_in = din("zd", [128, 32 * ICN], F32)
    bq_in = din("bq_p", [128, 32], F32)
    bk_in = din("bk_p", [128, 32], F32)
    bv_in = din("bv_p", [128, 32], F32)
    bo_in = din("bo_p", [128, 32], F32)
    gq_in = din("gq_c", [128, 1], F32)
    gk_in = din("gk_c", [128, 1], F32)
    id_in = din("ident_bf", [128, 128], BF16)
    idf_in = din("ident_f", [128, 128], F32)
    onb_in = din("ones_bf", [128, 128], BF16)
    onf_in = din("ones_f", [128, 128], F32)

    y_out = nc.dram_tensor("y", [T, D], F32, kind="ExternalOutput").ap()

    cck_in = nc.dram_tensor("cck_in", [D, 1024], BF16).ap()
    cck_out = nc.dram_tensor("cck_out", [4 * D, 1024], BF16).ap()
    ccv_in = nc.dram_tensor("ccv_in", [D, 1024], BF16).ap()
    ccv_out = nc.dram_tensor("ccv_out", [4 * D, 1024], BF16).ap()
    q_dram = nc.dram_tensor("q_dram", [D, T], BF16).ap()
    x1_dram = nc.dram_tensor("x1_dram", [T, D], F32).ap()
    hact_dram = nc.dram_tensor("hact_dram", [II, T], BF16).ap()
    wdt_dram = nc.dram_tensor("wdt_dram", [32, 128, II], BF16).ap()

    b.bar = b.sem("bar")
    ld_sy = b.sem("ld_sy")
    s_pe = b.sem("s_pe")
    s_v = b.sem("s_v")
    s_s = b.sem("s_s")
    s_gp = b.sem("s_gp")
    s_cc = b.sem("s_cc")
    ldx = b.semlist("ldx", 2)
    ldw = b.semlist("ldw", 2)
    stk = b.semlist("stk", 2)
    stv = b.semlist("stv", 4)
    ldk = b.semlist("ldk", 2)
    ldv = b.semlist("ldv", 2)
    ldq = b.semlist("ldq", 2)
    ldx4 = b.semlist("ldx4", 4)
    stx1 = b.semlist("stx1", 4)
    ldqg = b.semlist("ldqg", 2)
    ldqu = b.semlist("ldqu", 2)
    sth = b.semlist("sth", 2)
    ldqd = b.semlist("ldqd", 2)
    stw = b.semlist("stw", 2)
    ldb = b.semlist("ldb", 2)
    ldx1b = b.sem("ldx1b")
    sty4 = b.semlist("sty4", 4)

    ident = b.sbuf("ident_sb", [128, 128], BF16)
    ones_b = b.sbuf("ones_b_sb", [128, 128], BF16)
    ones_f = b.sbuf("ones_f_sb", [128, 128], F32)
    bq_sb = b.sbuf("bq_sb", [128, 32], F32)
    bk_sb = b.sbuf("bk_sb", [128, 32], F32)
    bv_sb = b.sbuf("bv_sb", [128, 32], F32)
    bo_sb = b.sbuf("bo_sb", [128, 32], F32)
    gq_sb = b.sbuf("gq_sb", [128, 1], F32)
    gk_sb = b.sbuf("gk_sb", [128, 1], F32)
    epsc = b.sbuf("epsc", [128, 2], F32)
    idf_in_sb = b.sbuf("idf_sb", [128, 128], F32)

    for dst, src in ((ident, id_in), (ones_b, onb_in), (ones_f, onf_in),
                     (bq_sb, bq_in), (bk_sb, bk_in), (bv_sb, bv_in),
                     (bo_sb, bo_in), (gk_sb, gk_in), (gq_sb, gq_in),
                     (idf_in_sb, idf_in)):
        b.dma("sync", dst[:], src, ld_sy)
    b.wait("vector", ld_sy)
    b.run("vector", lambda e: e.tensor_scalar(gq_sb[:], gq_sb[:], INV_SQRT_HD,
                                              None, ALU.mult), s_v)
    b.run("vector", lambda e: e.memset(epsc[:, 0:1], LN_EPS), s_v)
    b.run("vector", lambda e: e.memset(epsc[:, 1:2], RMS_EPS), s_v)
    b.barrier()

    def ln_transpose(src_dram, xnT_dst, sc, pre_wait=None):
        xt, xsq, xnb, ps = sc["xt"], sc["xsq"], sc["xnb"], sc["ps"]
        cols = sc["cols"]
        xnT_v = xnT_dst[:].rearrange("p (f t) -> p f t", f=DC)
        pe_done, ev_done, v_apply = [], [], []
        for tau in range(TC):
            par = tau % 2
            if tau == 0 and pre_wait is not None:
                for sem_, val_ in pre_wait:
                    b.wait("sync", sem_, val_)
            if tau >= 2:
                b.wait("sync", s_v, v_apply[tau - 2])
            b.dma("sync", xt[par][:], src_dram[tau * 128:(tau + 1) * 128, :],
                  ldx[par])
            ld_tgt = ldx[par].n
            b.wait("vector", ldx[par], ld_tgt)
            b.run("vector", lambda e, p=par: e.reduce_sum(
                cols[:, 0:1], xt[p][:], axis=AX), s_v)
            b.wait("scalar", ldx[par], ld_tgt)
            if tau >= 1:
                b.wait("scalar", s_v, v_apply[tau - 1])
            b.run("scalar", lambda e, p=par: e.activation(
                xsq[:], xt[p][:], AF.Square), s_s)
            sq_tgt = s_s.n
            b.wait("vector", s_s, sq_tgt)
            b.run("vector", lambda e: e.reduce_sum(cols[:, 2:3], xsq[:], axis=AX),
                  s_v)
            b.run("vector", lambda e: e.tensor_scalar(
                cols[:, 1:2], cols[:, 0:1], 1.0 / D, None, ALU.mult), s_v)
            b.run("vector", lambda e: e.tensor_scalar(
                cols[:, 3:4], cols[:, 2:3], 1.0 / D, None, ALU.mult), s_v)
            b.run("vector", lambda e: e.tensor_tensor(
                cols[:, 6:7], cols[:, 1:2], cols[:, 1:2], ALU.mult), s_v)
            b.run("vector", lambda e: e.tensor_scalar(
                cols[:, 6:7], cols[:, 6:7], -1.0, None, ALU.mult), s_v)
            b.run("vector", lambda e: e.tensor_tensor(
                cols[:, 3:4], cols[:, 3:4], cols[:, 6:7], ALU.add), s_v)
            var_tgt = s_v.n
            b.wait("scalar", s_v, var_tgt)
            b.run("scalar", lambda e: e.activation(
                cols[:, 4:5], cols[:, 3:4], AF.Sqrt, bias=epsc[:, 0:1]), s_s)
            std_tgt = s_s.n
            b.wait("vector", s_s, std_tgt)
            b.run("vector", lambda e: e.reciprocal(cols[:, 5:6], cols[:, 4:5]), s_v)
            if tau >= 2:
                b.wait("vector", s_pe, pe_done[tau - 2])
            b.run("vector", lambda e, p=par: e.tensor_scalar(
                xnb[p][:], xt[p][:], cols[:, 1:2], cols[:, 5:6],
                ALU.subtract, ALU.mult), s_v)
            v_apply.append(s_v.n)
            b.wait("tensor", s_v, v_apply[-1])
            for qd in range(8):
                reg = qd % 2
                gq_i = tau * 8 + qd
                if gq_i >= 2:
                    b.wait("tensor", s_s, ev_done[gq_i - 2])
                for j in range(4):
                    f = qd * 4 + j
                    b.run("tensor", lambda e, p=par, ff=f, r=reg, jj=j: e.matmul(
                        ps[:, r * 512 + jj * 128: r * 512 + (jj + 1) * 128],
                        xnb[p][:, ff * 128:(ff + 1) * 128], ident[:],
                        start=(jj == 0), stop=(jj == 3)),
                        s_pe if j == 3 else None)
                b.wait("scalar", s_pe, s_pe.n)
                b.run("scalar", lambda e, r=reg, qd_=qd, t_=tau: e.copy(
                    xnT_v[:, qd_ * 4:(qd_ + 1) * 4, t_ * 128:(t_ + 1) * 128],
                    ps[:, r * 512:(r + 1) * 512].rearrange(
                        "p (a c) -> p a c", a=4)), s_s)
                ev_done.append(s_s.n)
            pe_done.append(s_pe.n)

    with ExitStack() as sA:
        eA = sA.enter_context
        xnT = eA(nc.sbuf_tensor("xnT", [128, DC * T], BF16))

        with ExitStack() as sA1:
            e1 = sA1.enter_context
            sc1 = dict(
                xt=(e1(nc.sbuf_tensor("xt0", [128, D], F32)),
                    e1(nc.sbuf_tensor("xt1", [128, D], F32))),
                xsq=e1(nc.sbuf_tensor("xsq", [128, D], F32)),
                xnb=(e1(nc.sbuf_tensor("xnb0", [128, D], BF16)),
                     e1(nc.sbuf_tensor("xnb1", [128, D], BF16))),
                cols=e1(nc.sbuf_tensor("cols", [128, 8], F32)),
                ps=e1(nc.psum_tensor("ps_a", [128, 1024], F32)))
            ln_transpose(x_in, xnT, sc1)
        b.barrier()

        wsb = (eA(nc.sbuf_tensor("wsb0", [128, DC * 512], BF16)),
               eA(nc.sbuf_tensor("wsb1", [128, DC * 512], BF16)))
        kraw = [eA(nc.sbuf_tensor(f"kraw{i}", [128, 512], F32)) for i in range(4)]
        k2b = [eA(nc.sbuf_tensor(f"k2b{i}", [128, 512], BF16)) for i in range(2)]
        kp2 = [eA(nc.sbuf_tensor(f"kp2{i}", [128, 512], BF16)) for i in range(2)]
        tmpf = [eA(nc.sbuf_tensor(f"tmpf{i}", [128, 512], F32)) for i in range(2)]
        srow = [eA(nc.sbuf_tensor(f"srow{i}", [1, 512], F32)) for i in range(2)]
        rrow = [eA(nc.sbuf_tensor(f"rrow{i}", [1, 512], F32)) for i in range(2)]
        kbf = [eA(nc.sbuf_tensor(f"kbf{i}", [128, 1024], BF16)) for i in range(2)]
        vpc = [eA(nc.sbuf_tensor(f"vpc{i}", [128, 512], BF16)) for i in range(4)]
        ps_p = eA(nc.psum_tensor("ps_p", [128, 1024], F32))
        ps_r1 = eA(nc.psum_tensor("ps_r1", [128, 512], F32))
        ps_bc = eA(nc.psum_tensor("ps_bc", [128, 512], F32))
        ps_r2 = eA(nc.psum_tensor("ps_r2", [128, 512], F32))
        ps_v2 = eA(nc.psum_tensor("ps_v2", [128, 1024], F32))

        xnT_v = xnT[:].rearrange("p (f t) -> p f t", f=DC)

        def emit_proj(w_dram, bias_sb, g_col, out_kind):
            b.wait("tensor", s_s, s_s.n)
            b.wait("tensor", s_v, s_v.n)
            b.wait("scalar", s_v, s_v.n)
            b.wait("vector", s_s, s_s.n)
            b.wait("vector", stk[0], stk[0].n)
            b.wait("vector", stk[1], stk[1].n)

            pe_base, s_base, v_base = s_pe.n, s_s.n, s_v.n
            pe_entry = s_pe.n
            stk_base = [stk[0].n, stk[1].n]
            ldw_base = [ldw[0].n, ldw[1].n]

            def sq_inc(i):
                return s_base + 2 * i + 1
            def srow_inc(i):
                return s_base + 2 * i + 2
            v_kraw = [0] * 64
            v_rr = [0] * 64
            v_tt = [0] * 64
            v_kbf = [0] * 64
            cnt = v_base
            for i in range(64 + 3):
                if i < 64:
                    cnt += 1
                    v_kraw[i] = cnt
                if 0 <= i - 1 < 64:
                    cnt += 1
                    v_rr[i - 1] = cnt
                if 0 <= i - 3 < 64:
                    cnt += 2
                    v_tt[i - 3] = cnt - 1
                    v_kbf[i - 3] = cnt
            pe_proj = [0] * 64
            pe_sumsq = [0] * 64
            pe_bc = [0] * 64

            for i in range(64 + 4):
                if 0 <= i - 3 < 64:
                    j = i - 3
                    b.wait("tensor", s_v, v_rr[j])
                    if j >= 1:
                        b.wait("tensor", s_v, v_tt[j - 1])
                    b.run("tensor", lambda e, jj=j % 2: e.matmul(
                        ps_bc[:], ones_f[0:1, :], rrow[jj][0:1, :],
                        start=True, stop=True), s_pe)
                    pe_bc[j] = s_pe.n
                if i < 64:
                    h, th = divmod(i, 2)
                    p = h // 4
                    oc = h % 4
                    par = i % 2
                    if th == 0 and oc == 0:
                        b.wait("tensor", ldw[p % 2],
                               ldw_base[p % 2] + 16 * (p // 2 + 1))
                    if i >= 2:
                        b.wait("tensor", s_v, v_kraw[i - 2])
                    for f in range(DC):
                        b.run("tensor", lambda e, pp=p, o=oc, tt=th, ff=f, pr=par:
                              e.matmul(ps_p[:, pr * 512:(pr + 1) * 512],
                                       wsb[pp % 2][:, ff * 512 + o * 128:
                                                   ff * 512 + (o + 1) * 128],
                                       xnT_v[:, ff, tt * 512:(tt + 1) * 512],
                                       start=(ff == 0), stop=(ff == DC - 1)),
                              s_pe if f == DC - 1 else None)
                    pe_proj[i] = s_pe.n
                if 0 <= i - 1 < 64:
                    j = i - 1
                    b.wait("tensor", s_s, sq_inc(j))
                    if j >= 2:
                        b.wait("tensor", s_s, srow_inc(j - 2))
                    b.run("tensor", lambda e, jj=j % 2: e.matmul(
                        ps_r1[0:1, :], ones_b[:, 0:1], k2b[jj][:],
                        start=True, stop=True), s_pe)
                    pe_sumsq[j] = s_pe.n

            for i in range(64):
                b.wait("scalar", s_v, v_kraw[i])
                if i >= 2:
                    b.wait("scalar", s_pe, pe_sumsq[i - 2])
                b.run("scalar", lambda e, i4=i % 4, pr=i % 2: e.activation(
                    k2b[pr][:], kraw[i4][:], AF.Square), s_s)
                b.wait("scalar", s_pe, pe_sumsq[i])
                if i >= 2:
                    b.wait("scalar", s_v, v_rr[i - 2])
                b.run("scalar", lambda e, jj=i % 2: e.activation(
                    srow[jj][0:1, :], ps_r1[0:1, :], AF.Sqrt, scale=1.0 / HD,
                    bias=epsc[0:1, 1:2]), s_s)

            for i in range(64 + 3):
                if i < 64:
                    h, th = divmod(i, 2)
                    b.wait("vector", s_pe, pe_proj[i])
                    if i >= 4:
                        b.wait("vector", s_s, sq_inc(i - 4))
                    b.run("vector", lambda e, hh=h, pr=i % 2, i4=i % 4:
                          e.tensor_scalar(
                              kraw[i4][:], ps_p[:, pr * 512:(pr + 1) * 512],
                              bias_sb[:, hh:hh + 1], None, ALU.add), s_v)
                if 0 <= i - 1 < 64:
                    j = i - 1
                    b.wait("vector", s_s, srow_inc(j))
                    if j >= 2:
                        b.wait("vector", s_pe, pe_bc[j - 2])
                    b.run("vector", lambda e, jj=j % 2: e.reciprocal(
                        rrow[jj][0:1, :], srow[jj][0:1, :]), s_v)
                if 0 <= i - 3 < 64:
                    j = i - 3
                    h, th = divmod(j, 2)
                    b.wait("vector", s_pe, pe_bc[j])
                    b.run("vector", lambda e, i4=j % 4, pr=j % 2: e.tensor_tensor(
                        tmpf[pr][:], kraw[i4][:], ps_bc[:], ALU.mult), s_v)
                    if j >= 4:
                        b.wait("vector", stk[h % 2],
                               stk_base[h % 2] + 16 * (h // 2))
                    b.run("vector", lambda e, hh=h, tt=th, pr=j % 2:
                          e.tensor_scalar(
                              kbf[hh % 2][:, tt * 512:(tt + 1) * 512],
                              tmpf[pr][:], g_col[:, 0:1], None, ALU.mult), s_v)

            w_re = w_dram.rearrange("(f p) o -> p f o", p=128)
            for p in range(8):
                if p >= 2:
                    b.wait("gpsimd", s_pe, pe_proj[(p - 2) * 8 + 7])
                else:
                    b.wait("gpsimd", s_pe, pe_entry)
                b.dma("gpsimd", wsb[p % 2][:].rearrange("p (f o) -> p f o", f=DC),
                      w_re[:, :, p * 512:(p + 1) * 512], ldw[p % 2])

            for h in range(H):
                b.wait("sync", s_v, v_kbf[2 * h + 1])
                dst = (cck_in[h * 128:(h + 1) * 128, :] if out_kind == "k"
                       else q_dram[h * 128:(h + 1) * 128, :])
                b.dma("sync", dst, kbf[h % 2][:], stk[h % 2])

        emit_proj(wk, bk_sb, gk_sb, "k")
        for s_ in stk:
            b.wait("gpsimd", s_, s_.n)
        b.run("gpsimd", lambda e: e.collective_compute(
            "AllGather", ALU.bypass,
            replica_groups=[[0, 1, 2, 3], [4, 5, 6, 7]],
            ins=[cck_in.opt()], outs=[cck_out.opt()]), s_cc, 1)

        b.wait("tensor", s_s, s_s.n)
        b.wait("vector", s_s, s_s.n)
        v_pe_base, v_v_base = s_pe.n, s_v.n
        v_st_base = [s.n for s in stv]
        v_ld_base = [ldw[0].n, ldw[1].n]
        def vg_pe(gi):
            return v_pe_base + gi + 1
        def vg_v(gi, t2):
            return v_v_base + gi * 2 + t2 + 1
        w_re = wv.rearrange("(f p) o -> p f o", p=128)
        for p in range(8):
            if p >= 2:
                b.wait("gpsimd", s_pe, vg_pe((p - 1) * 4 - 1))
            else:
                b.wait("gpsimd", s_pe, v_pe_base)
            b.dma("gpsimd", wsb[p % 2][:].rearrange("p (f o) -> p f o", f=DC),
                  w_re[:, :, p * 512:(p + 1) * 512], ldw[p % 2])
        for p in range(8):
            b.wait("tensor", ldw[p % 2], v_ld_base[p % 2] + 16 * (p // 2 + 1))
            for tcg in range(4):
                gi = p * 4 + tcg
                if gi >= 1:
                    b.wait("tensor", s_v, vg_v(gi - 1, 1))
                for f in range(DC):
                    for t2 in range(2):
                        tc = tcg * 2 + t2
                        b.run("tensor", lambda e, ff=f, t2_=t2, tc_=tc, pp=p:
                              e.matmul(ps_v2[:, t2_ * 512:(t2_ + 1) * 512],
                                       xnT_v[:, ff, tc_ * 128:(tc_ + 1) * 128],
                                       wsb[pp % 2][:, ff * 512:(ff + 1) * 512],
                                       start=(ff == 0), stop=(ff == DC - 1)),
                              s_pe if (f == DC - 1 and t2 == 1) else None)
                pe_tgt = s_pe.n
                for t2 in range(2):
                    tc = tcg * 2 + t2
                    b.wait("vector", s_pe, pe_tgt)
                    vslot = (gi * 2 + t2) % 4
                    if gi * 2 + t2 >= 4:
                        b.wait("vector", stv[vslot],
                               v_st_base[vslot] + 16 * ((gi * 2 + t2) // 4))
                    b.run("vector", lambda e, sl=vslot, t2_=t2: e.tensor_copy(
                        vpc[sl][:], ps_v2[:, t2_ * 512:(t2_ + 1) * 512]), s_v)
                    b.wait("sync", s_v, s_v.n)
                    off = (p // 2) * 1024 + (p % 2) * 512
                    dst = ccv_in[tc * 512:(tc + 1) * 512, :].rearrange(
                        "(t f) c -> t (f c)", t=128)[:, off:off + 512]
                    b.dma("sync", dst, vpc[vslot][:], stv[vslot])

        for s_ in stv:
            b.wait("gpsimd", s_, s_.n)
        b.run("gpsimd", lambda e: e.collective_compute(
            "AllGather", ALU.bypass,
            replica_groups=[[0, 1, 2, 3], [4, 5, 6, 7]],
            ins=[ccv_in.opt()], outs=[ccv_out.opt()]), s_cc, 1)

        emit_proj(wq, bq_sb, gq_sb, "q")
        q_st_final = [stk[0].n, stk[1].n]
        b.barrier()

    with ExitStack() as sB:
        eB = sB.enter_context
        ao = eB(nc.sbuf_tensor("ao", [128, H * T], BF16))

        with ExitStack() as sB1:
            e4 = sB1.enter_context
            probsT = [e4(nc.sbuf_tensor(f"probsT{i}", [128, DC * 512], BF16))
                      for i in range(2)]
            khb = [e4(nc.sbuf_tensor(f"khb{i}", [128, 4096], BF16))
                   for i in range(2)]
            vhb = [e4(nc.sbuf_tensor(f"vhb{i}", [128, 4096], BF16))
                   for i in range(2)]
            qhb = [e4(nc.sbuf_tensor(f"qhb{i}", [128, 1024], BF16))
                   for i in range(2)]
            recb = [e4(nc.sbuf_tensor(f"recb{i}", [128, 512], F32))
                    for i in range(2)]
            tta = [e4(nc.sbuf_tensor(f"tta{i}", [128, 512], F32))
                   for i in range(2)]
            grow = e4(nc.sbuf_tensor("grow", [1, 256], F32))
            nbcol = e4(nc.sbuf_tensor("nbcol", [128, 1], F32))
            srow2 = e4(nc.sbuf_tensor("srow2", [1, 512], F32))
            rrow2 = e4(nc.sbuf_tensor("rrow2", [1, 512], F32))
            ps_sc = e4(nc.psum_tensor("ps_sc", [128, 2048], F32))
            ps_pv = e4(nc.psum_tensor("ps_pv", [128, 1024], F32))
            ps_sum = e4(nc.psum_tensor("ps_sum", [128, 512], F32))
            ps_m = e4(nc.psum_tensor("ps_m", [128, 512], F32))

            cc_r = cck_out.rearrange("(r z) c -> r z c", r=4)

            b.run("tensor", lambda e: e.matmul(
                ps_m[0:1, 0:128], gq_sb[:, 0:1], idf_in_sb[:],
                start=True, stop=True), s_pe)
            b.run("tensor", lambda e: e.matmul(
                ps_m[0:1, 128:256], gk_sb[:, 0:1], idf_in_sb[:],
                start=True, stop=True), s_pe)
            gt_tgt = s_pe.n
            b.wait("scalar", s_pe, gt_tgt)
            b.run("scalar", lambda e: e.copy(grow[0:1, :], ps_m[0:1, 0:256]), s_s)
            g_cp = s_s.n
            b.wait("vector", s_s, g_cp)
            b.run("vector", lambda e: e.tensor_tensor(
                grow[0:1, :], grow[0:1, :], grow[0:1, :], ALU.mult), s_v)
            b.run("vector", lambda e: e.reduce_max(
                grow[0:1, 0:1], grow[0:1, 0:128], axis=AX), s_v)
            b.run("vector", lambda e: e.reduce_max(
                grow[0:1, 1:2], grow[0:1, 128:256], axis=AX), s_v)
            b.run("vector", lambda e: e.tensor_tensor(
                grow[0:1, 0:1], grow[0:1, 0:1], grow[0:1, 1:2], ALU.mult), s_v)
            g_m2 = s_v.n
            b.wait("scalar", s_v, g_m2)
            b.run("scalar", lambda e: e.activation(
                grow[0:1, 2:3], grow[0:1, 0:1], AF.Sqrt), s_s)
            g_sq = s_s.n
            b.wait("vector", s_s, g_sq)
            b.run("vector", lambda e: e.tensor_scalar(
                grow[0:1, 2:3], grow[0:1, 2:3], -128.0, None, ALU.mult), s_v)
            nb_tgt = s_v.n
            b.wait("tensor", s_v, nb_tgt)
            b.run("tensor", lambda e: e.matmul(
                ps_m[:, 0:1], ones_f[0:1, :], grow[0:1, 2:3],
                start=True, stop=True), s_pe)
            bc_all = s_pe.n
            b.wait("scalar", s_pe, bc_all)
            b.run("scalar", lambda e: e.copy(nbcol[:], ps_m[:, 0:1]), s_s)
            nbcol_tgt = s_s.n

            pe_base = s_pe.n
            import os
            if os.environ.get("KDBG"):
                print("P4 pe_base", pe_base, "s_base", s_s.n, "v_base", s_v.n)
            per_nh = 19
            def sc_pair_inc(h, nh, mcg):
                return pe_base + (h * 2 + nh) * per_nh + mcg + 1
            def sums_inc(h, nh):
                return pe_base + (h * 2 + nh) * per_nh + 17
            def pv_inc(h, nh):
                return pe_base + (h * 2 + nh) * per_nh + 18
            def bc2_inc(h, nh):
                return pe_base + (h * 2 + nh) * per_nh + 19
            s_base = s_s.n
            def exp_inc(h, nh, mcg):
                return s_base + (h * 2 + nh) * 18 + mcg + 1
            def srow_cp_inc(h, nh):
                return s_base + (h * 2 + nh) * 18 + 17
            def rec_inc(h, nh):
                return s_base + (h * 2 + nh) * 18 + 18
            v_base = s_v.n
            def rr2_inc(h, nh):
                return v_base + (h * 2 + nh) * 3 + 1
            def aot_inc(h, nh):
                return v_base + (h * 2 + nh) * 3 + 2
            def aow_inc(h, nh):
                return v_base + (h * 2 + nh) * 3 + 3
            ldk_base = [ldk[0].n, ldk[1].n]
            ldv_base = [ldv[0].n, ldv[1].n]
            ldq_base = [ldq[0].n, ldq[1].n]

            b.wait("sync", s_cc, s_cc.n)
            b.wait("sync", stk[0], q_st_final[0])
            b.wait("sync", stk[1], q_st_final[1])
            for h in range(H):
                if h >= 2:
                    b.wait("sync", s_pe, pv_inc(h - 2, 1))
                b.dma("sync", khb[h % 2][:].rearrange("p (r t) -> p r t", r=4),
                      cc_r[:, h * 128:(h + 1) * 128, :].transpose([1, 0, 2]),
                      ldk[h % 2])
                for r in range(4):
                    off = (h // 8) * 1024 + (h % 8) * 128
                    vsrc = ccv_out[r * D:(r + 1) * D, :].rearrange(
                        "(tt t f) c -> tt t (f c)", tt=TC, t=128)[
                        :, :, off:off + 128].transpose([1, 0, 2])
                    b.dma("sync",
                          vhb[h % 2][:, r * 1024:(r + 1) * 1024].rearrange(
                              "p (tt d) -> p tt d", tt=TC),
                          vsrc, ldv[h % 2])
                b.dma("sync", qhb[h % 2][:], q_dram[h * 128:(h + 1) * 128, :],
                      ldq[h % 2])

            for h in range(H):
                b.wait("tensor", ldk[h % 2], ldk_base[h % 2] + 16 * (h // 2 + 1))
                b.wait("tensor", ldv[h % 2], ldv_base[h % 2] + 64 * (h // 2 + 1))
                b.wait("tensor", ldq[h % 2], ldq_base[h % 2] + 16 * (h // 2 + 1))
                for nh in range(2):
                    for mcg in range(16):
                        reg = mcg % 2
                        gidx = (h * 2 + nh) * 16 + mcg
                        if gidx >= 2:
                            pm = gidx - 2
                            hh, rem = divmod(pm, 32)
                            nn, mm = divmod(rem, 16)
                            b.wait("tensor", s_s, exp_inc(hh, nn, mm))
                        for j in range(2):
                            mc = mcg * 2 + j
                            b.run("tensor",
                                  lambda e, hh=h, n_=nh, m_=mc, r=reg, jj=j:
                                  e.matmul(ps_sc[:, r * 1024 + jj * 512:
                                                 r * 1024 + (jj + 1) * 512],
                                           khb[hh % 2][:, m_ * 128:(m_ + 1) * 128],
                                           qhb[hh % 2][:, n_ * 512:(n_ + 1) * 512],
                                           start=True, stop=True),
                                  s_pe if j == 1 else None)
                    b.wait("tensor", s_s, exp_inc(h, nh, 15))
                    if h * 2 + nh >= 1:
                        pi = h * 2 + nh - 1
                        b.wait("tensor", s_s, srow_cp_inc(pi // 2, pi % 2))
                    for mc in range(DC):
                        b.run("tensor", lambda e, m_=mc, pb=(h * 2 + nh) % 2:
                              e.matmul(
                                  ps_sum[0:1, :], ones_b[:, 0:1],
                                  probsT[pb][:, m_ * 512:(m_ + 1) * 512],
                                  start=(m_ == 0), stop=(m_ == DC - 1)),
                              s_pe if mc == DC - 1 else None)
                    if h * 2 + nh >= 2:
                        pi = h * 2 + nh - 2
                        b.wait("tensor", s_v, aot_inc(pi // 2, pi % 2))
                    for mc in range(DC):
                        b.run("tensor", lambda e, hh=h, n_=nh, m_=mc,
                              pb=(h * 2 + nh) % 2: e.matmul(
                                  ps_pv[:, (n_ % 2) * 512:(n_ % 2 + 1) * 512],
                                  vhb[hh % 2][:, m_ * 128:(m_ + 1) * 128],
                                  probsT[pb][:, m_ * 512:(m_ + 1) * 512],
                                  start=(m_ == 0), stop=(m_ == DC - 1)),
                              s_pe if mc == DC - 1 else None)
                    b.wait("tensor", s_v, rr2_inc(h, nh))
                    if h * 2 + nh >= 1:
                        pi = h * 2 + nh - 1
                        b.wait("tensor", s_s, rec_inc(pi // 2, pi % 2))
                    else:
                        b.wait("tensor", s_s, nbcol_tgt)
                    b.run("tensor", lambda e: e.matmul(
                        ps_m[:], ones_f[0:1, :], rrow2[0:1, :],
                        start=True, stop=True), s_pe)

                for nh in range(2):
                    for mcg in range(16):
                        b.wait("scalar", s_pe, sc_pair_inc(h, nh, mcg))
                        if mcg == 0:
                            u = h * 2 + nh
                            if u >= 2:
                                pu = u - 2
                                b.wait("scalar", s_pe,
                                       pv_inc(pu // 2, pu % 2))
                            else:
                                b.wait("scalar", s_s, nbcol_tgt)
                        b.run("scalar", lambda e, m_=mcg, pb=(h * 2 + nh) % 2:
                              e.activation(
                                  probsT[pb][:, m_ * 1024:(m_ + 1) * 1024],
                                  ps_sc[:, (m_ % 2) * 1024:(m_ % 2 + 1) * 1024],
                                  AF.Exp, bias=nbcol[:, 0:1]), s_s)
                    b.wait("scalar", s_pe, sums_inc(h, nh))
                    if h * 2 + nh >= 1:
                        pi = h * 2 + nh - 1
                        b.wait("scalar", s_v, rr2_inc(pi // 2, pi % 2))
                    b.run("scalar", lambda e: e.copy(srow2[0:1, :],
                                                     ps_sum[0:1, :]), s_s)
                    b.wait("scalar", s_pe, bc2_inc(h, nh))
                    b.run("scalar", lambda e, n_=nh: e.copy(recb[n_][:],
                                                            ps_m[:]), s_s)

                for nh in range(2):
                    b.wait("vector", s_s, srow_cp_inc(h, nh))
                    if h * 2 + nh >= 1:
                        pi = h * 2 + nh - 1
                        b.wait("vector", s_pe, bc2_inc(pi // 2, pi % 2))
                    b.run("vector", lambda e: e.reciprocal(rrow2[0:1, :],
                                                           srow2[0:1, :]), s_v)
                    b.wait("vector", s_pe, pv_inc(h, nh))
                    b.wait("vector", s_s, rec_inc(h, nh))
                    b.run("vector", lambda e, n_=nh: e.tensor_tensor(
                        tta[n_][:], ps_pv[:, (n_ % 2) * 512:(n_ % 2 + 1) * 512],
                        recb[n_][:], ALU.mult), s_v)
                    b.run("vector", lambda e, hh=h, n_=nh: e.tensor_scalar(
                        ao[:, hh * 1024 + n_ * 512:hh * 1024 + (n_ + 1) * 512],
                        tta[n_][:], bv_sb[:, hh:hh + 1], None, ALU.add), s_v)
        b.barrier()

        with ExitStack() as sB2:
            e5 = sB2.enter_context
            wob = [e5(nc.sbuf_tensor(f"wob{i}", [128, DC * 512], BF16))
                   for i in range(2)]
            opan = [e5(nc.sbuf_tensor(f"opan{i}", [128, 4096], BF16))
                    for i in range(2)]
            xsl = [e5(nc.sbuf_tensor(f"xsl{i}", [128, 512], F32))
                   for i in range(4)]
            x1p = [e5(nc.sbuf_tensor(f"x1p{i}", [128, 512], F32))
                   for i in range(4)]
            ps_o = e5(nc.psum_tensor("ps_o", [128, 1024], F32))
            ps_t5 = e5(nc.psum_tensor("ps_t5", [128, 1024], F32))

            pe5, s5, v5 = s_pe.n, s_s.n, s_v.n
            ld5 = [s.n for s in ldx4]
            ldg5 = [ldw[0].n, ldw[1].n]
            st5 = [s.n for s in stx1]
            def o_proj_inc(p, i8):
                return pe5 + p * 16 + i8 + 1
            def o_tq_inc(p, tau):
                return pe5 + p * 16 + 8 + tau + 1
            def o_ev_inc(p, i8):
                return v5 + p * 16 + i8 + 1
            def x1_inc(p, tau):
                return v5 + p * 16 + 8 + tau + 1

            wo_re = wo.rearrange("(f p) o -> p f o", p=128)
            for p in range(8):
                if p >= 2:
                    b.wait("gpsimd", s_pe, o_proj_inc(p - 2, 7))
                b.dma("gpsimd", wob[p % 2][:].rearrange("p (f o) -> p f o", f=DC),
                      wo_re[:, :, p * 512:(p + 1) * 512], ldw[p % 2])
            for p in range(8):
                for tau in range(8):
                    gi = p * 8 + tau
                    if gi >= 4:
                        pj = gi - 4
                        b.wait("sync", s_v, x1_inc(pj // 8, pj % 8))
                    b.dma("sync", xsl[gi % 4][:],
                          x_in[tau * 128:(tau + 1) * 128,
                               p * 512:(p + 1) * 512], ldx4[gi % 4])
            for p in range(8):
                b.wait("tensor", ldw[p % 2], ldg5[p % 2] + 16 * (p // 2 + 1))
                for i8 in range(8):
                    oc, th = divmod(i8, 2)
                    gi8 = p * 8 + i8
                    if gi8 >= 2:
                        pj = gi8 - 2
                        b.wait("tensor", s_v, o_ev_inc(pj // 8, pj % 8))
                    for hc in range(DC):
                        b.run("tensor", lambda e, pp=p, o=oc, tt=th, c=hc, r=i8 % 2:
                              e.matmul(ps_o[:, r * 512:(r + 1) * 512],
                                       wob[pp % 2][:, c * 512 + o * 128:
                                                   c * 512 + (o + 1) * 128],
                                       ao[:, c * 1024 + tt * 512:
                                          c * 1024 + (tt + 1) * 512],
                                       start=(c == 0), stop=(c == DC - 1)),
                              s_pe if hc == DC - 1 else None)
                b.wait("tensor", s_v, o_ev_inc(p, 7))
                for tau in range(8):
                    gq_i = p * 8 + tau
                    if gq_i >= 2:
                        pj = gq_i - 2
                        b.wait("tensor", s_v, x1_inc(pj // 8, pj % 8))
                    for oc in range(4):
                        b.run("tensor", lambda e, pp=p, t_=tau, o=oc: e.matmul(
                            ps_t5[:, (t_ % 2) * 512 + o * 128:
                                  (t_ % 2) * 512 + (o + 1) * 128],
                            opan[pp % 2][:, o * 1024 + t_ * 128:
                                         o * 1024 + (t_ + 1) * 128],
                            ident[:], start=(o == 0), stop=(o == 3)),
                            s_pe if oc == 3 else None)
            for p in range(8):
                for i8 in range(8):
                    oc, th = divmod(i8, 2)
                    b.wait("vector", s_pe, o_proj_inc(p, i8))
                    if p >= 2 and i8 == 0:
                        b.wait("vector", s_pe, o_tq_inc(p - 2, 7))
                    b.run("vector", lambda e, pp=p, o=oc, tt=th, r=i8 % 2:
                          e.tensor_scalar(
                              opan[pp % 2][:, o * 1024 + tt * 512:
                                           o * 1024 + (tt + 1) * 512],
                              ps_o[:, r * 512:(r + 1) * 512],
                              bo_sb[:, pp * 4 + o:pp * 4 + o + 1], None,
                              ALU.add), s_v)
                for tau in range(8):
                    gi = p * 8 + tau
                    b.wait("vector", s_pe, o_tq_inc(p, tau))
                    b.wait("vector", ldx4[gi % 4], ld5[gi % 4] + 16 * (gi // 4 + 1))
                    if gi >= 4:
                        b.wait("vector", stx1[gi % 4], st5[gi % 4] + 16 * (gi // 4))
                    b.run("vector", lambda e, t_=tau, g=gi: e.tensor_tensor(
                        x1p[g % 4][:],
                        ps_t5[:, (t_ % 2) * 512:((t_ % 2) + 1) * 512],
                        xsl[g % 4][:], ALU.add), s_v)
            for p in range(8):
                for tau in range(8):
                    gi = p * 8 + tau
                    b.wait("scalar", s_v, x1_inc(p, tau))
                    b.dma("scalar",
                          x1_dram[tau * 128:(tau + 1) * 128,
                                  p * 512:(p + 1) * 512],
                          x1p[gi % 4][:], stx1[gi % 4])
        b.barrier()
    x1_st_final = [s.n for s in stx1]

    with ExitStack() as sD:
        eD = sD.enter_context
        xn2T = eD(nc.sbuf_tensor("xn2T", [128, DC * T], BF16))

        with ExitStack() as sD1:
            e6 = sD1.enter_context
            sc2 = dict(
                xt=(e6(nc.sbuf_tensor("xt0b", [128, D], F32)),
                    e6(nc.sbuf_tensor("xt1b", [128, D], F32))),
                xsq=e6(nc.sbuf_tensor("xsqb", [128, D], F32)),
                xnb=(e6(nc.sbuf_tensor("xnb0b", [128, D], BF16)),
                     e6(nc.sbuf_tensor("xnb1b", [128, D], BF16))),
                cols=e6(nc.sbuf_tensor("colsb", [128, 8], F32)),
                ps=e6(nc.psum_tensor("ps_a2", [128, 1024], F32)))
            ln_transpose(x1_dram, xn2T, sc2,
                         pre_wait=list(zip(stx1, x1_st_final)))
        b.barrier()

        sgz = {}
        for nm, src in (("sg", sg_in), ("zg", zg_in), ("su", su_in),
                        ("zu", zu_in)):
            sgz[nm] = eD(nc.sbuf_tensor(nm + "_sb", [128, ICN * 32], F32))
            b.dma("sync", sgz[nm][:], src, ld_sy)
        b.wait("vector", ld_sy)
        for zn, sn in (("zg", "sg"), ("zu", "su")):
            b.run("vector", lambda e, z=sgz[zn], s=sgz[sn]: e.tensor_tensor(
                z[:], z[:], s[:], ALU.mult), s_v)
            b.run("vector", lambda e, z=sgz[zn]: e.tensor_scalar(
                z[:], z[:], -1.0, None, ALU.mult), s_v)
        nz_tgt = s_v.n

        qg_sb = [eD(nc.sbuf_tensor(f"qg{i}", [128, D], I32)) for i in range(2)]
        qu_sb = [eD(nc.sbuf_tensor(f"qu{i}", [128, D], I32)) for i in range(2)]
        dqg = [eD(nc.sbuf_tensor(f"dqg{i}", [128, 128], BF16)) for i in range(8)]
        dqu = [eD(nc.sbuf_tensor(f"dqu{i}", [128, 128], BF16)) for i in range(8)]
        wgT = eD(nc.sbuf_tensor("wgT", [128, D], BF16))
        wuT = eD(nc.sbuf_tensor("wuT", [128, D], BF16))
        gsb = [eD(nc.sbuf_tensor(f"gsb{i}", [128, 512], BF16)) for i in range(2)]
        usb = [eD(nc.sbuf_tensor(f"usb{i}", [128, 512], BF16)) for i in range(2)]
        hac = [eD(nc.sbuf_tensor(f"hac{i}", [128, 1024], BF16)) for i in range(2)]
        ps_wt = eD(nc.psum_tensor("ps_wt", [128, 2048], F32))
        ps_g = eD(nc.psum_tensor("ps_g", [128, 1024], F32))
        ps_u = eD(nc.psum_tensor("ps_u", [128, 1024], F32))

        xn2T_v = xn2T[:].rearrange("p (f t) -> p f t", f=DC)
        pe6 = s_pe.n
        def wgq_inc(ic, qd):
            return pe6 + ic * 20 + qd + 1
        def wuq_inc(ic, qd):
            return pe6 + ic * 20 + 8 + qd + 1
        def g_inc(ic, th):
            return pe6 + ic * 20 + 16 + th + 1
        def u_inc(ic, th):
            return pe6 + ic * 20 + 18 + th + 1
        s6 = s_s.n
        def wge_inc(ic, e):
            return s6 + ic * 12 + e + 1
        def wue_inc(ic, e):
            return s6 + ic * 12 + 4 + e + 1
        def silu_inc(ic, th):
            return s6 + ic * 12 + 8 + 2 * th + 1
        def uev_inc(ic, th):
            return s6 + ic * 12 + 8 + 2 * th + 2
        v6 = s_v.n
        import os
        if os.environ.get("KDBG"):
            print("P6 pe6", pe6, "s6", s6, "v6", v6, "g6_pre", s_gp.n)
        def vdq_inc(ic, hc):
            if ic == 0:
                return v6 + hc + 1
            start = v6 + 32 + 34 * (ic - 1)
            return start + hc + 1 + (2 if hc >= 16 else 0)
        def hm_inc(ic, th):
            return v6 + 32 + 34 * ic + 16 + th + 1
        g6 = s_gp.n
        def pdq_inc(ic, hc):
            return g6 + ic * 32 + hc + 1
        ld6g = [ldqg[0].n, ldqg[1].n]
        ld6u = [ldqu[0].n, ldqu[1].n]
        st6 = [sth[0].n, sth[1].n]

        for ic in range(ICN):
            if ic >= 2:
                b.wait("sync", s_v, vdq_inc(ic - 2, 31))
            b.dma("sync", qg_sb[ic % 2][:], qwg[ic * 128:(ic + 1) * 128, :],
                  ldqg[ic % 2])
            if ic >= 2:
                b.wait("sync", s_gp, pdq_inc(ic - 2, 31))
            b.dma("sync", qu_sb[ic % 2][:], qwu[ic * 128:(ic + 1) * 128, :],
                  ldqu[ic % 2])

            b.wait("vector", ldqg[ic % 2], ld6g[ic % 2] + 16 * (ic // 2 + 1))
            if ic == 0:
                b.wait("vector", s_v, nz_tgt)
            for hc in range(DC):
                if hc == 16 and ic >= 1:
                    for th in range(2):
                        b.wait("vector", s_s, uev_inc(ic - 1, th))
                        if ic >= 3:
                            par_h = (ic - 1) % 2
                            b.wait("vector", sth[par_h],
                                   st6[par_h] + 16 * ((ic - 1) // 2))
                        b.run("vector", lambda e, i=ic - 1, tt=th:
                              e.tensor_tensor(
                                  hac[i % 2][:, tt * 512:(tt + 1) * 512],
                                  gsb[tt][:], usb[tt][:], ALU.mult), s_v)
                gt = ic * 32 + hc
                if gt >= 8:
                    gj = gt - 8
                    b.wait("vector", s_pe, wgq_inc(gj // 32, (gj % 32) // 4))
                b.run("vector", lambda e, i=ic, c=hc: e.tensor_scalar(
                    dqg[c % 8][:], qg_sb[i % 2][:, c * 128:(c + 1) * 128],
                    sgz["sg"][:, i * 32 + c:i * 32 + c + 1],
                    sgz["zg"][:, i * 32 + c:i * 32 + c + 1],
                    ALU.mult, ALU.add), s_v)
            b.wait("gpsimd", ldqu[ic % 2], ld6u[ic % 2] + 16 * (ic // 2 + 1))
            if ic == 0:
                b.wait("gpsimd", s_v, nz_tgt)
            for hc in range(DC):
                gt = ic * 32 + hc
                if gt >= 8:
                    gj = gt - 8
                    b.wait("gpsimd", s_pe, wuq_inc(gj // 32, (gj % 32) // 4))
                b.run("gpsimd", lambda e, i=ic, c=hc: e.tensor_scalar(
                    dqu[c % 8][:], qu_sb[i % 2][:, c * 128:(c + 1) * 128],
                    sgz["su"][:, i * 32 + c:i * 32 + c + 1],
                    sgz["zu"][:, i * 32 + c:i * 32 + c + 1],
                    ALU.mult, ALU.add), s_gp)
            for qd in range(8):
                if qd >= 4:
                    b.wait("tensor", s_s, wge_inc(ic, (qd - 4) // 2))
                elif ic >= 1:
                    b.wait("tensor", s_s, wue_inc(ic - 1, 2 + qd // 2))
                for j in range(4):
                    hc = qd * 4 + j
                    b.wait("tensor", s_v, vdq_inc(ic, hc))
                    b.run("tensor", lambda e, c=hc, jj=j, q_=qd: e.matmul(
                        ps_wt[:, (q_ % 4) * 512 + jj * 128:
                              (q_ % 4) * 512 + (jj + 1) * 128],
                        dqg[c % 8][:], ident[:], start=(jj == 0), stop=(jj == 3)),
                        s_pe if j == 3 else None)
            for qd in range(8):
                if qd >= 4:
                    b.wait("tensor", s_s, wue_inc(ic, (qd - 4) // 2))
                else:
                    b.wait("tensor", s_s, wge_inc(ic, 2 + qd // 2))
                for j in range(4):
                    hc = qd * 4 + j
                    b.wait("tensor", s_gp, pdq_inc(ic, hc))
                    b.run("tensor", lambda e, c=hc, jj=j, q_=qd: e.matmul(
                        ps_wt[:, (q_ % 4) * 512 + jj * 128:
                              (q_ % 4) * 512 + (jj + 1) * 128],
                        dqu[c % 8][:], ident[:], start=(jj == 0), stop=(jj == 3)),
                        s_pe if j == 3 else None)
            b.wait("tensor", s_s, wge_inc(ic, 3))
            for th in range(2):
                if ic >= 1:
                    b.wait("tensor", s_s, silu_inc(ic - 1, th))
                for hc in range(DC):
                    b.run("tensor", lambda e, c=hc, tt=th: e.matmul(
                        ps_g[:, tt * 512:(tt + 1) * 512],
                        wgT[:, c * 128:(c + 1) * 128],
                        xn2T_v[:, c, tt * 512:(tt + 1) * 512],
                        start=(c == 0), stop=(c == DC - 1)),
                        s_pe if hc == DC - 1 else None)
            b.wait("tensor", s_s, wue_inc(ic, 3))
            for th in range(2):
                if ic >= 1:
                    b.wait("tensor", s_s, uev_inc(ic - 1, th))
                for hc in range(DC):
                    b.run("tensor", lambda e, c=hc, tt=th: e.matmul(
                        ps_u[:, tt * 512:(tt + 1) * 512],
                        wuT[:, c * 128:(c + 1) * 128],
                        xn2T_v[:, c, tt * 512:(tt + 1) * 512],
                        start=(c == 0), stop=(c == DC - 1)),
                        s_pe if hc == DC - 1 else None)
            for e_ in range(4):
                b.wait("scalar", s_pe, wgq_inc(ic, 2 * e_ + 1))
                b.run("scalar", lambda e, q_=2 * e_: e.copy(
                    wgT[:, q_ * 512:(q_ + 2) * 512],
                    ps_wt[:, (q_ % 4) * 512:((q_ % 4) + 2) * 512]), s_s)
            for e_ in range(4):
                b.wait("scalar", s_pe, wuq_inc(ic, 2 * e_ + 1))
                b.run("scalar", lambda e, q_=2 * e_: e.copy(
                    wuT[:, q_ * 512:(q_ + 2) * 512],
                    ps_wt[:, (q_ % 4) * 512:((q_ % 4) + 2) * 512]), s_s)
            for th in range(2):
                b.wait("scalar", s_pe, g_inc(ic, th))
                if ic >= 1:
                    b.wait("scalar", s_v, hm_inc(ic - 1, th))
                b.run("scalar", lambda e, tt=th: e.activation(
                    gsb[tt][:], ps_g[:, tt * 512:(tt + 1) * 512], AF.Silu), s_s)
                b.wait("scalar", s_pe, u_inc(ic, th))
                b.run("scalar", lambda e, tt=th: e.activation(
                    usb[tt][:], ps_u[:, tt * 512:(tt + 1) * 512], AF.Copy), s_s)
            if ic >= 1:
                b.wait("scalar", s_v, hm_inc(ic - 1, 1))
                b.dma("scalar", hact_dram[(ic - 1) * 128:ic * 128, :],
                      hac[(ic - 1) % 2][:], sth[(ic - 1) % 2])
        par_t = (ICN - 1) % 2
        b.wait("vector", sth[par_t], st6[par_t] + 16 * ((ICN - 1 - par_t) // 2))
        for th in range(2):
            b.wait("vector", s_s, uev_inc(ICN - 1, th))
            b.run("vector", lambda e, i=ICN - 1, tt=th: e.tensor_tensor(
                hac[i % 2][:, tt * 512:(tt + 1) * 512], gsb[tt][:],
                usb[tt][:], ALU.mult), s_v)
        b.wait("scalar", s_v, s_v.n)
        b.dma("scalar", hact_dram[(ICN - 1) * 128:ICN * 128, :],
              hac[(ICN - 1) % 2][:], sth[(ICN - 1) % 2])
        b.barrier()

    with ExitStack() as sF:
        eF = sF.enter_context
        szd = {}
        for nm, src in (("sd", sd_in), ("zd", zd_in)):
            szd[nm] = eF(nc.sbuf_tensor(nm + "_sb", [128, 32 * ICN], F32))
            b.dma("sync", szd[nm][:], src, ld_sy)
        b.wait("vector", ld_sy)
        b.run("vector", lambda e: e.tensor_tensor(
            szd["zd"][:], szd["zd"][:], szd["sd"][:], ALU.mult), s_v)
        b.run("vector", lambda e: e.tensor_scalar(
            szd["zd"][:], szd["zd"][:], -1.0, None, ALU.mult), s_v)
        nzd_tgt = s_v.n

        qd_sb = [eF(nc.sbuf_tensor(f"qd{i}", [128, II], I32)) for i in range(2)]
        dqd = [eF(nc.sbuf_tensor(f"dqd{i}", [128, 128], BF16)) for i in range(8)]
        wdc = [eF(nc.sbuf_tensor(f"wdc{i}", [128, II], BF16)) for i in range(2)]
        ps_w7 = eF(nc.psum_tensor("ps_w7", [128, 2048], F32))

        groups = [list(range(g * 4, min(g * 4 + 4, ICN))) for g in range(22)]
        v_ics = [ic for ic in range(ICN) if ic % 4 != 3]
        p_ics = [ic for ic in range(ICN) if ic % 4 == 3]
        v_ord = {ic: i for i, ic in enumerate(v_ics)}
        p_ord = {ic: i for i, ic in enumerate(p_ics)}
        NV, NP = len(v_ics), len(p_ics)
        pe7 = s_pe.n
        def w7q_inc(hc, g):
            return pe7 + hc * 22 + g + 1
        s7 = s_s.n
        def w7e_inc(hc, e):
            return s7 + hc * 11 + e + 1
        v7 = s_v.n
        g7 = s_gp.n
        def dq7_sem_val(hc, ic):
            if ic % 4 == 3:
                return s_gp, g7 + hc * NP + p_ord[ic] + 1
            return s_v, v7 + hc * NV + v_ord[ic] + 1
        ld7 = [ldqd[0].n, ldqd[1].n]
        st7 = [stw[0].n, stw[1].n]

        for hc in range(32):
            if hc >= 2:
                b.wait("sync", s_v, v7 + (hc - 1) * NV)
                b.wait("sync", s_gp, g7 + (hc - 1) * NP)
            b.dma("sync", qd_sb[hc % 2][:], qwd[hc * 128:(hc + 1) * 128, :],
                  ldqd[hc % 2])
            if hc >= 1:
                b.wait("scalar", s_s, w7e_inc(hc - 1, 10))
                b.dma("scalar", wdt_dram[hc - 1, :, :], wdc[(hc - 1) % 2][:],
                      stw[(hc - 1) % 2])
            for eng, ics_e in (("vector", v_ics), ("gpsimd", p_ics)):
                b.wait(eng, ldqd[hc % 2], ld7[hc % 2] + 16 * (hc // 2 + 1))
                if hc == 0:
                    b.wait(eng, s_v, nzd_tgt)
                sem = s_v if eng == "vector" else s_gp
                for ic in ics_e:
                    if ic >= 8:
                        b.wait(eng, s_pe, w7q_inc(hc, (ic - 8) // 4))
                    elif hc >= 1:
                        b.wait(eng, s_pe, w7q_inc(hc - 1, 21))
                    b.run(eng, lambda e, h_=hc, i=ic: e.tensor_scalar(
                        dqd[i % 8][:], qd_sb[h_ % 2][:, i * 128:(i + 1) * 128],
                        szd["sd"][:, h_ * ICN + i:h_ * ICN + i + 1],
                        szd["zd"][:, h_ * ICN + i:h_ * ICN + i + 1],
                        ALU.mult, ALU.add), sem)
            for g, ics in enumerate(groups):
                G = hc * 22 + g
                if G >= 4:
                    pj = G - 4
                    b.wait("tensor", s_s, w7e_inc(pj // 22, (pj % 22) // 2))
                for j, ic in enumerate(ics):
                    dsem, dval = dq7_sem_val(hc, ic)
                    b.wait("tensor", dsem, dval)
                    b.run("tensor", lambda e, i=ic, jj=j, r=G % 4, n_=len(ics):
                          e.matmul(ps_w7[:, r * 512 + jj * 128:
                                         r * 512 + (jj + 1) * 128],
                                   dqd[i % 8][:], ident[:],
                                   start=(jj == 0), stop=(jj == n_ - 1)),
                          s_pe if j == len(ics) - 1 else None)
                if g % 2 == 1:
                    e_ = g // 2
                    b.wait("scalar", s_pe, w7q_inc(hc, g))
                    if e_ == 0 and hc >= 2:
                        b.wait("scalar", stw[hc % 2],
                               st7[hc % 2] + 16 * (hc // 2))
                    g0 = 2 * e_
                    width = (len(groups[g0]) + len(groups[g0 + 1])) * 128
                    r0 = (hc * 22 + g0) % 4
                    b.run("scalar", lambda e, h_=hc, c0=g0 * 512, w_=width,
                          rr=r0: e.copy(
                              wdc[h_ % 2][:, c0:c0 + w_],
                              ps_w7[:, rr * 512:rr * 512 + w_]), s_s)
        b.wait("scalar", s_s, s_s.n)
        b.dma("scalar", wdt_dram[31, :, :], wdc[1][:], stw[1])
        b.barrier()
    wdt_st_final = [stw[0].n, stw[1].n]

    with ExitStack() as sG:
        eG = sG.enter_context
        wdts = [eG(nc.sbuf_tensor(f"wdts{i}", [128, 8 * 512], BF16))
                for i in range(2)]
        hsl = [eG(nc.sbuf_tensor(f"hsl{i}", [128, 4 * 512], BF16))
               for i in range(2)]
        dnsb = eG(nc.sbuf_tensor("dnsb", [128, 8 * 512], BF16))
        x1s = [eG(nc.sbuf_tensor(f"x1s{i}", [128, 1024], F32)) for i in range(4)]
        ypc = [eG(nc.sbuf_tensor(f"ypc{i}", [128, 1024], F32)) for i in range(4)]
        ps_d = eG(nc.psum_tensor("ps_d", [128, 8 * 512], F32))

        hact_r = hact_dram.rearrange("(ic p) t -> ic p t", p=128)
        icbs = [list(range(g * 4, min(g * 4 + 4, ICN))) for g in range(22)]
        pe8 = s_pe.n
        def b_inc(pss, bi):
            return pe8 + pss * 30 + bi + 1
        def tq8_inc(pss, tau, half):
            return pe8 + pss * 30 + 22 + tau * 2 + half + 1
        s8 = s_s.n
        def dne_inc(pss, hc8):
            return s8 + pss * 8 + hc8 + 1
        v8 = s_v.n
        def y_inc(pss, tau):
            return v8 + pss * 8 + 2 * tau + 2
        ld8b = [ldb[0].n, ldb[1].n]
        ld8x = ldx1b.n

        b.wait("sync", stw[0], wdt_st_final[0])
        b.wait("sync", stw[1], wdt_st_final[1])
        b.wait("sync", sth[0], sth[0].n)
        b.wait("sync", sth[1], sth[1].n)
        for s_, v_ in zip(stx1, x1_st_final):
            b.wait("sync", s_, v_)
        for pss in range(8):
            th, grp = divmod(pss, 4)
            for bi, ics in enumerate(icbs):
                gi = pss * 22 + bi
                if gi >= 2:
                    pj = gi - 2
                    b.wait("sync", s_pe, b_inc(pj // 22, pj % 22))
                w = len(ics)
                b.dma("sync",
                      wdts[bi % 2][:, 0:8 * w * 128].rearrange(
                          "p (c i) -> p c i", c=8),
                      wdt_dram[grp * 8:(grp + 1) * 8, :,
                               ics[0] * 128:ics[0] * 128 + w * 128]
                      .transpose([1, 0, 2]), ldb[bi % 2])
                b.dma("sync",
                      hsl[bi % 2][:, 0:w * 512].rearrange(
                          "p (i t) -> p i t", i=w),
                      hact_r[ics[0]:ics[0] + w, :, th * 512:(th + 1) * 512]
                      .transpose([1, 0, 2]), ldb[bi % 2])
            for tau in range(4):
                if pss >= 1:
                    b.wait("sync", s_v, y_inc(pss - 1, tau))
                b.dma("sync", x1s[tau][:],
                      x1_dram[th * 512 + tau * 128:th * 512 + (tau + 1) * 128,
                              grp * 1024:(grp + 1) * 1024], ldx1b)
            for bi, ics in enumerate(icbs):
                gb = pss * 22 + bi
                b.wait("tensor", ldb[bi % 2], ld8b[bi % 2] + 32 * (gb // 2 + 1))
                if bi == 0 and pss >= 1:
                    b.wait("tensor", s_v, y_inc(pss - 1, 3))
                w = len(ics)
                for j, ic in enumerate(ics):
                    for hc8 in range(8):
                        b.run("tensor", lambda e, b_=bi, jj=j, c8=hc8, i=ic,
                              w_=w:
                              e.matmul(ps_d[:, c8 * 512:(c8 + 1) * 512],
                                       wdts[b_ % 2][:, c8 * w_ * 128 + jj * 128:
                                                    c8 * w_ * 128 + (jj + 1) * 128],
                                       hsl[b_ % 2][:, jj * 512:(jj + 1) * 512],
                                       start=(i == 0), stop=(i == ICN - 1)),
                              s_pe if (j == w - 1 and hc8 == 7) else None)
            for hc8 in range(8):
                b.wait("scalar", s_pe, b_inc(pss, 21))
                b.run("scalar", lambda e, c8=hc8: e.copy(
                    dnsb[:, c8 * 512:(c8 + 1) * 512],
                    ps_d[:, c8 * 512:(c8 + 1) * 512]), s_s)
            for tau in range(4):
                for half in range(2):
                    b.wait("tensor", s_s, dne_inc(pss, 7))
                    if tau * 2 + half >= 2:
                        b.wait("tensor", s_v, y_inc(pss, tau - 1))
                    for j in range(4):
                        hc8 = half * 4 + j
                        b.run("tensor", lambda e, c8=hc8, jj=j, hf=half, t_=tau:
                              e.matmul(ps_d[:, hf * 512 + jj * 128:
                                            hf * 512 + (jj + 1) * 128],
                                       dnsb[:, c8 * 512 + t_ * 128:
                                            c8 * 512 + (t_ + 1) * 128],
                                       ident[:], start=(jj == 0), stop=(jj == 3)),
                              s_pe if j == 3 else None)
                b.wait("vector", s_pe, tq8_inc(pss, tau, 1))
                b.wait("vector", ldx1b, ld8x + 64 * (pss + 1))
                if pss >= 1:
                    b.wait("vector", sty4[tau], 16 * pss)
                for half in range(2):
                    b.run("vector", lambda e, t_=tau, hf=half: e.tensor_tensor(
                        ypc[t_][:, hf * 512:(hf + 1) * 512],
                        ps_d[:, hf * 512:(hf + 1) * 512],
                        x1s[t_][:, hf * 512:(hf + 1) * 512], ALU.add), s_v)
                b.wait("scalar", s_v, y_inc(pss, tau))
                b.dma("scalar",
                      y_out[th * 512 + tau * 128:th * 512 + (tau + 1) * 128,
                            grp * 1024:(grp + 1) * 1024],
                      ypc[tau][:], sty4[tau])
        for s_ in sty4:
            b.wait("sync", s_, s_.n)
        b.barrier(drain=False)

    b.finalize_block()
    b.ctx.close()
    return nc


_NC_CACHE = None


def kernel(**inputs):
    global _NC_CACHE
    xs = np.ascontiguousarray(np.asarray(inputs["L0_attn_hidden_states"]))
    ident = np.eye(128, dtype=ml_dtypes.bfloat16)
    ones_b = np.ones((128, 128), dtype=ml_dtypes.bfloat16)
    ones_f = np.ones((128, 128), dtype=np.float32)

    def pcol(v):
        return np.ascontiguousarray(np.asarray(v, np.float32).reshape(32, 128).T)

    def prow_gu(v):
        return np.ascontiguousarray(
            np.asarray(v, np.float32).reshape(ICN, 128, 32)
            .transpose(1, 0, 2).reshape(128, -1))

    def prow_d(v):
        return np.ascontiguousarray(
            np.asarray(v, np.float32).reshape(32, 128, ICN)
            .transpose(1, 0, 2).reshape(128, -1))

    common = dict(
        wq=np.ascontiguousarray(np.asarray(inputs["L0_attn_Wq"], np.float32)),
        wk=np.ascontiguousarray(np.asarray(inputs["L0_attn_Wk"], np.float32)),
        wv=np.ascontiguousarray(np.asarray(inputs["L0_attn_Wv"], np.float32)),
        wo=np.ascontiguousarray(np.asarray(inputs["L0_attn_Wo"], np.float32)),
        qwg=np.ascontiguousarray(np.asarray(inputs["L0_ffn_qw_gate"], np.int32)),
        qwu=np.ascontiguousarray(np.asarray(inputs["L0_ffn_qw_up"], np.int32)),
        qwd=np.ascontiguousarray(np.asarray(inputs["L0_ffn_qw_down"], np.int32)),
        sg=prow_gu(inputs["L0_ffn_s_gate"]), zg=prow_gu(inputs["L0_ffn_z_gate"]),
        su=prow_gu(inputs["L0_ffn_s_up"]), zu=prow_gu(inputs["L0_ffn_z_up"]),
        sd=prow_d(inputs["L0_ffn_s_down"]), zd=prow_d(inputs["L0_ffn_z_down"]),
        bq_p=pcol(inputs["L0_attn_bq"]), bk_p=pcol(inputs["L0_attn_bk"]),
        bv_p=pcol(inputs["L0_attn_bv"]), bo_p=pcol(inputs["L0_attn_bo"]),
        gq_c=np.ascontiguousarray(
            np.asarray(inputs["L0_attn_gq"], np.float32).reshape(128, 1)),
        gk_c=np.ascontiguousarray(
            np.asarray(inputs["L0_attn_gk"], np.float32).reshape(128, 1)),
        ident_bf=ident, ones_bf=ones_b, ones_f=ones_f,
        ident_f=np.eye(128, dtype=np.float32),
    )
    in_maps = []
    for c in range(8):
        bb, r = divmod(c, 4)
        m = dict(common)
        m["x_in"] = np.ascontiguousarray(xs[bb, r * T:(r + 1) * T, :], np.float32)
        in_maps.append(m)

    if _NC_CACHE is None:
        _NC_CACHE = build_program()
    res = run_bass_kernel_spmd(_NC_CACHE, in_maps, core_ids=list(range(8)))
    out = np.empty((B, N, D), np.float32)
    for c in range(8):
        bb, r = divmod(c, 4)
        out[bb, r * T:(r + 1) * T, :] = res.results[c]["y"]
    return out


# version 06
# speedup vs baseline: 1.1834x; 1.0214x over previous
import math
from contextlib import ExitStack

import numpy as np
import ml_dtypes

import concourse.bass as bass
import concourse.mybir as mybir
from concourse.bass_utils import run_bass_kernel_spmd

F32 = mybir.dt.float32
BF16 = mybir.dt.bfloat16
I32 = mybir.dt.int32
AF = mybir.ActivationFunctionType
ALU = mybir.AluOpType
AX = mybir.AxisListType.X

B, N, D = 2, 4096, 4096
H, HD = 32, 128
II = 11008
T = 1024
TC = T // 128
DC = D // 128
ICN = II // 128
RS = 2 * D
INV_SQRT_HD = 1.0 / math.sqrt(HD)
ENGS = ("sync", "tensor", "vector", "scalar", "gpsimd")
LN_EPS = 1e-5
RMS_EPS = 1e-6


class Sem:
    __slots__ = ("h", "n")

    def __init__(self, h):
        self.h = h
        self.n = 0


class Builder:
    def __init__(self):
        self.nc = bass.Bass("TRN2", target_bir_lowering=False)
        self.q = {e: [] for e in ENGS}
        self.ctx = ExitStack()
        self.bar_k = 0

    def sem(self, name):
        return Sem(self.ctx.enter_context(self.nc.semaphore(name)))

    def semlist(self, name, k):
        return [self.sem(f"{name}{i}") for i in range(k)]

    def sbuf(self, name, shape, dtype):
        return self.ctx.enter_context(self.nc.sbuf_tensor(name, shape, dtype))

    def emit(self, eng, fn):
        self.q[eng].append(fn)

    def wait(self, eng, sem, val=None):
        v = sem.n if val is None else val
        if v > 0:
            self.emit(eng, lambda e, s=sem.h, vv=v: e.wait_ge(s, vv))

    SELF_SYNC = ("vector", "scalar", "gpsimd")

    def run(self, eng, build, sem=None, k=1):
        if eng in self.SELF_SYNC and sem is not None and sem.n > 0:
            self.wait(eng, sem, sem.n)
        if sem is not None:
            sem.n += k
        def f(e, bld=build, s=sem, kk=k):
            ins = bld(e)
            if s is not None:
                ins.then_inc(s.h, kk)
        self.emit(eng, f)
        return sem.n if sem is not None else None

    def dma(self, eng, out, in_, sem, **kw):
        return self.run(eng, lambda e, o=out, i=in_, k=kw: e.dma_start(out=o, in_=i, **k),
                        sem, 16)

    def barrier(self, drain=True):
        self.bar_k += 1
        tgt = 5 * self.bar_k
        for eng in ENGS:
            if drain and eng in ("tensor", "vector", "scalar", "gpsimd"):
                self.emit(eng, lambda e: e.drain())
            self.emit(eng, lambda e, s=self.bar.h: e.sem_inc(s, 1))
            self.emit(eng, lambda e, s=self.bar.h, v=tgt: e.wait_ge(s, v))

    def finalize_block(self):
        nc = self.nc
        with nc.Block() as blk:
            for eng in ENGS:
                fns = self.q[eng]
                getattr(blk, eng)(lambda e, fns=fns: [fn(e) for fn in fns])


def build_program():
    b = Builder()
    nc = b.nc

    def din(name, shape, dtype):
        return nc.dram_tensor(name, shape, dtype, kind="ExternalInput").ap()

    x_in = din("x_in", [T, D], F32)
    wq = din("wq", [D, D], F32)
    wk = din("wk", [D, D], F32)
    wv = din("wv", [D, D], F32)
    wo = din("wo", [D, D], F32)
    qwg = din("qwg", [II, D], I32)
    qwu = din("qwu", [II, D], I32)
    qwd = din("qwd", [D, II], I32)
    sg_in = din("sg", [128, ICN * 32], F32)
    zg_in = din("zg", [128, ICN * 32], F32)
    su_in = din("su", [128, ICN * 32], F32)
    zu_in = din("zu", [128, ICN * 32], F32)
    sd_in = din("sd", [128, 32 * ICN], F32)
    zd_in = din("zd", [128, 32 * ICN], F32)
    bq_in = din("bq_p", [128, 32], F32)
    bk_in = din("bk_p", [128, 32], F32)
    bv_in = din("bv_p", [128, 32], F32)
    bo_in = din("bo_p", [128, 32], F32)
    gq_in = din("gq_c", [128, 1], F32)
    gk_in = din("gk_c", [128, 1], F32)
    id_in = din("ident_bf", [128, 128], BF16)
    idf_in = din("ident_f", [128, 128], F32)
    onb_in = din("ones_bf", [128, 128], BF16)
    onf_in = din("ones_f", [128, 128], F32)

    y_out = nc.dram_tensor("y", [T, D], F32, kind="ExternalOutput").ap()

    cck_in = nc.dram_tensor("cck_in", [D, 1024], BF16).ap()
    cck_out = nc.dram_tensor("cck_out", [4 * D, 1024], BF16).ap()
    ccv_in = nc.dram_tensor("ccv_in", [D, 1024], BF16).ap()
    ccv_out = nc.dram_tensor("ccv_out", [4 * D, 1024], BF16).ap()
    q_dram = nc.dram_tensor("q_dram", [D, T], BF16).ap()
    x1_dram = nc.dram_tensor("x1_dram", [T, D], F32).ap()
    hact_dram = nc.dram_tensor("hact_dram", [II, T], BF16).ap()
    wdt_dram = nc.dram_tensor("wdt_dram", [32, 128, II], BF16).ap()

    b.bar = b.sem("bar")
    ld_sy = b.sem("ld_sy")
    s_pe = b.sem("s_pe")
    s_v = b.sem("s_v")
    s_s = b.sem("s_s")
    s_gp = b.sem("s_gp")
    s_cc = b.sem("s_cc")
    ldx = b.semlist("ldx", 2)
    ldw = b.semlist("ldw", 2)
    stk = b.semlist("stk", 2)
    stv = b.semlist("stv", 4)
    ldk = b.semlist("ldk", 2)
    ldv = b.semlist("ldv", 2)
    ldq = b.semlist("ldq", 2)
    ldx4 = b.semlist("ldx4", 4)
    stx1 = b.semlist("stx1", 4)
    ldqg = b.semlist("ldqg", 2)
    ldqu = b.semlist("ldqu", 2)
    sth = b.semlist("sth", 2)
    ldqd = b.semlist("ldqd", 2)
    stw = b.semlist("stw", 2)
    ldb = b.semlist("ldb", 2)
    ldx1b = b.sem("ldx1b")
    sty4 = b.semlist("sty4", 4)
    trg = b.sem("trg")
    tru = b.sem("tru")
    trd = b.sem("trd")
    trx = b.sem("trx")
    tro = b.sem("tro")
    trd7 = b.sem("trd7")

    ident = b.sbuf("ident_sb", [128, 128], BF16)
    ones_b = b.sbuf("ones_b_sb", [128, 128], BF16)
    ones_f = b.sbuf("ones_f_sb", [128, 128], F32)
    bq_sb = b.sbuf("bq_sb", [128, 32], F32)
    bk_sb = b.sbuf("bk_sb", [128, 32], F32)
    bv_sb = b.sbuf("bv_sb", [128, 32], F32)
    bo_sb = b.sbuf("bo_sb", [128, 32], F32)
    gq_sb = b.sbuf("gq_sb", [128, 1], F32)
    gk_sb = b.sbuf("gk_sb", [128, 1], F32)
    epsc = b.sbuf("epsc", [128, 2], F32)
    idf_in_sb = b.sbuf("idf_sb", [128, 128], F32)

    for dst, src in ((ident, id_in), (ones_b, onb_in), (ones_f, onf_in),
                     (bq_sb, bq_in), (bk_sb, bk_in), (bv_sb, bv_in),
                     (bo_sb, bo_in), (gk_sb, gk_in), (gq_sb, gq_in),
                     (idf_in_sb, idf_in)):
        b.dma("sync", dst[:], src, ld_sy)
    b.wait("vector", ld_sy)
    b.run("vector", lambda e: e.tensor_scalar(gq_sb[:], gq_sb[:], INV_SQRT_HD,
                                              None, ALU.mult), s_v)
    b.run("vector", lambda e: e.memset(epsc[:, 0:1], LN_EPS), s_v)
    b.run("vector", lambda e: e.memset(epsc[:, 1:2], RMS_EPS), s_v)
    b.barrier()

    def ln_transpose(src_dram, xnT_dst, sc, pre_wait=None):
        xt, xsq, xnb, ps = sc["xt"], sc["xsq"], sc["xnb"], sc["ps"]
        cols = sc["cols"]
        xnT_v = xnT_dst[:].rearrange("p (f t) -> p f t", f=DC)
        pe_done, ev_done, v_apply = [], [], []
        for tau in range(TC):
            par = tau % 2
            if tau == 0 and pre_wait is not None:
                for sem_, val_ in pre_wait:
                    b.wait("sync", sem_, val_)
            if tau >= 2:
                b.wait("sync", s_v, v_apply[tau - 2])
            b.dma("sync", xt[par][:], src_dram[tau * 128:(tau + 1) * 128, :],
                  ldx[par])
            ld_tgt = ldx[par].n
            b.wait("vector", ldx[par], ld_tgt)
            b.run("vector", lambda e, p=par: e.reduce_sum(
                cols[:, 0:1], xt[p][:], axis=AX), s_v)
            b.wait("scalar", ldx[par], ld_tgt)
            if tau >= 1:
                b.wait("scalar", s_v, v_apply[tau - 1])
            b.run("scalar", lambda e, p=par: e.activation(
                xsq[:], xt[p][:], AF.Square), s_s)
            sq_tgt = s_s.n
            b.wait("vector", s_s, sq_tgt)
            b.run("vector", lambda e: e.reduce_sum(cols[:, 2:3], xsq[:], axis=AX),
                  s_v)
            b.run("vector", lambda e: e.tensor_scalar(
                cols[:, 1:2], cols[:, 0:1], 1.0 / D, None, ALU.mult), s_v)
            b.run("vector", lambda e: e.tensor_scalar(
                cols[:, 3:4], cols[:, 2:3], 1.0 / D, None, ALU.mult), s_v)
            b.run("vector", lambda e: e.tensor_tensor(
                cols[:, 6:7], cols[:, 1:2], cols[:, 1:2], ALU.mult), s_v)
            b.run("vector", lambda e: e.tensor_scalar(
                cols[:, 6:7], cols[:, 6:7], -1.0, None, ALU.mult), s_v)
            b.run("vector", lambda e: e.tensor_tensor(
                cols[:, 3:4], cols[:, 3:4], cols[:, 6:7], ALU.add), s_v)
            var_tgt = s_v.n
            b.wait("scalar", s_v, var_tgt)
            b.run("scalar", lambda e: e.activation(
                cols[:, 4:5], cols[:, 3:4], AF.Sqrt, bias=epsc[:, 0:1]), s_s)
            std_tgt = s_s.n
            b.wait("vector", s_s, std_tgt)
            b.run("vector", lambda e: e.reciprocal(cols[:, 5:6], cols[:, 4:5]), s_v)
            if tau >= 2:
                b.wait("vector", s_pe, pe_done[tau - 2])
            b.run("vector", lambda e, p=par: e.tensor_scalar(
                xnb[p][:], xt[p][:], cols[:, 1:2], cols[:, 5:6],
                ALU.subtract, ALU.mult), s_v)
            v_apply.append(s_v.n)
            b.wait("tensor", s_v, v_apply[-1])
            for qd in range(8):
                reg = qd % 2
                gq_i = tau * 8 + qd
                if gq_i >= 2:
                    b.wait("tensor", s_s, ev_done[gq_i - 2])
                for j in range(4):
                    f = qd * 4 + j
                    b.run("tensor", lambda e, p=par, ff=f, r=reg, jj=j: e.matmul(
                        ps[:, r * 512 + jj * 128: r * 512 + (jj + 1) * 128],
                        xnb[p][:, ff * 128:(ff + 1) * 128], ident[:],
                        start=(jj == 0), stop=(jj == 3)),
                        s_pe if j == 3 else None)
                b.wait("scalar", s_pe, s_pe.n)
                b.run("scalar", lambda e, r=reg, qd_=qd, t_=tau: e.copy(
                    xnT_v[:, qd_ * 4:(qd_ + 1) * 4, t_ * 128:(t_ + 1) * 128],
                    ps[:, r * 512:(r + 1) * 512].rearrange(
                        "p (a c) -> p a c", a=4)), s_s)
                ev_done.append(s_s.n)
            pe_done.append(s_pe.n)

    with ExitStack() as sA:
        eA = sA.enter_context
        xnT = eA(nc.sbuf_tensor("xnT", [128, DC * T], BF16))

        with ExitStack() as sA1:
            e1 = sA1.enter_context
            sc1 = dict(
                xt=(e1(nc.sbuf_tensor("xt0", [128, D], F32)),
                    e1(nc.sbuf_tensor("xt1", [128, D], F32))),
                xsq=e1(nc.sbuf_tensor("xsq", [128, D], F32)),
                xnb=(e1(nc.sbuf_tensor("xnb0", [128, D], BF16)),
                     e1(nc.sbuf_tensor("xnb1", [128, D], BF16))),
                cols=e1(nc.sbuf_tensor("cols", [128, 8], F32)),
                ps=e1(nc.psum_tensor("ps_a", [128, 1024], F32)))
            ln_transpose(x_in, xnT, sc1)
        b.barrier()

        wsb = (eA(nc.sbuf_tensor("wsb0", [128, DC * 512], BF16)),
               eA(nc.sbuf_tensor("wsb1", [128, DC * 512], BF16)))
        kraw = [eA(nc.sbuf_tensor(f"kraw{i}", [128, 512], F32)) for i in range(4)]
        k2b = [eA(nc.sbuf_tensor(f"k2b{i}", [128, 512], BF16)) for i in range(2)]
        kp2 = [eA(nc.sbuf_tensor(f"kp2{i}", [128, 512], BF16)) for i in range(2)]
        tmpf = [eA(nc.sbuf_tensor(f"tmpf{i}", [128, 512], F32)) for i in range(2)]
        srow = [eA(nc.sbuf_tensor(f"srow{i}", [1, 512], F32)) for i in range(2)]
        rrow = [eA(nc.sbuf_tensor(f"rrow{i}", [1, 512], F32)) for i in range(2)]
        kbf = [eA(nc.sbuf_tensor(f"kbf{i}", [128, 1024], BF16)) for i in range(2)]
        vpc = [eA(nc.sbuf_tensor(f"vpc{i}", [128, 512], BF16)) for i in range(4)]
        ps_p = eA(nc.psum_tensor("ps_p", [128, 1024], F32))
        ps_r1 = eA(nc.psum_tensor("ps_r1", [128, 512], F32))
        ps_bc = eA(nc.psum_tensor("ps_bc", [128, 512], F32))
        ps_r2 = eA(nc.psum_tensor("ps_r2", [128, 512], F32))
        ps_v2 = eA(nc.psum_tensor("ps_v2", [128, 1024], F32))

        xnT_v = xnT[:].rearrange("p (f t) -> p f t", f=DC)

        def emit_proj(w_dram, bias_sb, g_col, out_kind):
            b.wait("tensor", s_s, s_s.n)
            b.wait("tensor", s_v, s_v.n)
            b.wait("scalar", s_v, s_v.n)
            b.wait("vector", s_s, s_s.n)
            b.wait("vector", stk[0], stk[0].n)
            b.wait("vector", stk[1], stk[1].n)

            pe_base, s_base, v_base = s_pe.n, s_s.n, s_v.n
            pe_entry = s_pe.n
            stk_base = [stk[0].n, stk[1].n]
            ldw_base = [ldw[0].n, ldw[1].n]

            def sq_inc(i):
                return s_base + 2 * i + 1
            def srow_inc(i):
                return s_base + 2 * i + 2
            v_kraw = [0] * 64
            v_rr = [0] * 64
            v_tt = [0] * 64
            v_kbf = [0] * 64
            cnt = v_base
            for i in range(64 + 3):
                if i < 64:
                    cnt += 1
                    v_kraw[i] = cnt
                if 0 <= i - 1 < 64:
                    cnt += 1
                    v_rr[i - 1] = cnt
                if 0 <= i - 3 < 64:
                    cnt += 2
                    v_tt[i - 3] = cnt - 1
                    v_kbf[i - 3] = cnt
            pe_proj = [0] * 64
            pe_sumsq = [0] * 64
            pe_bc = [0] * 64

            for i in range(64 + 4):
                if 0 <= i - 3 < 64:
                    j = i - 3
                    b.wait("tensor", s_v, v_rr[j])
                    if j >= 1:
                        b.wait("tensor", s_v, v_tt[j - 1])
                    b.run("tensor", lambda e, jj=j % 2: e.matmul(
                        ps_bc[:], ones_f[0:1, :], rrow[jj][0:1, :],
                        start=True, stop=True), s_pe)
                    pe_bc[j] = s_pe.n
                if i < 64:
                    h, th = divmod(i, 2)
                    p = h // 4
                    oc = h % 4
                    par = i % 2
                    if th == 0 and oc == 0:
                        b.wait("tensor", ldw[p % 2],
                               ldw_base[p % 2] + 16 * (p // 2 + 1))
                    if i >= 2:
                        b.wait("tensor", s_v, v_kraw[i - 2])
                    for f in range(DC):
                        b.run("tensor", lambda e, pp=p, o=oc, tt=th, ff=f, pr=par:
                              e.matmul(ps_p[:, pr * 512:(pr + 1) * 512],
                                       wsb[pp % 2][:, ff * 512 + o * 128:
                                                   ff * 512 + (o + 1) * 128],
                                       xnT_v[:, ff, tt * 512:(tt + 1) * 512],
                                       start=(ff == 0), stop=(ff == DC - 1)),
                              s_pe if f == DC - 1 else None)
                    pe_proj[i] = s_pe.n
                if 0 <= i - 1 < 64:
                    j = i - 1
                    b.wait("tensor", s_s, sq_inc(j))
                    if j >= 2:
                        b.wait("tensor", s_s, srow_inc(j - 2))
                    b.run("tensor", lambda e, jj=j % 2: e.matmul(
                        ps_r1[0:1, :], ones_b[:, 0:1], k2b[jj][:],
                        start=True, stop=True), s_pe)
                    pe_sumsq[j] = s_pe.n

            for i in range(64):
                b.wait("scalar", s_v, v_kraw[i])
                if i >= 2:
                    b.wait("scalar", s_pe, pe_sumsq[i - 2])
                b.run("scalar", lambda e, i4=i % 4, pr=i % 2: e.activation(
                    k2b[pr][:], kraw[i4][:], AF.Square), s_s)
                b.wait("scalar", s_pe, pe_sumsq[i])
                if i >= 2:
                    b.wait("scalar", s_v, v_rr[i - 2])
                b.run("scalar", lambda e, jj=i % 2: e.activation(
                    srow[jj][0:1, :], ps_r1[0:1, :], AF.Sqrt, scale=1.0 / HD,
                    bias=epsc[0:1, 1:2]), s_s)

            for i in range(64 + 3):
                if i < 64:
                    h, th = divmod(i, 2)
                    b.wait("vector", s_pe, pe_proj[i])
                    if i >= 4:
                        b.wait("vector", s_s, sq_inc(i - 4))
                    b.run("vector", lambda e, hh=h, pr=i % 2, i4=i % 4:
                          e.tensor_scalar(
                              kraw[i4][:], ps_p[:, pr * 512:(pr + 1) * 512],
                              bias_sb[:, hh:hh + 1], None, ALU.add), s_v)
                if 0 <= i - 1 < 64:
                    j = i - 1
                    b.wait("vector", s_s, srow_inc(j))
                    if j >= 2:
                        b.wait("vector", s_pe, pe_bc[j - 2])
                    b.run("vector", lambda e, jj=j % 2: e.reciprocal(
                        rrow[jj][0:1, :], srow[jj][0:1, :]), s_v)
                if 0 <= i - 3 < 64:
                    j = i - 3
                    h, th = divmod(j, 2)
                    b.wait("vector", s_pe, pe_bc[j])
                    b.run("vector", lambda e, i4=j % 4, pr=j % 2: e.tensor_tensor(
                        tmpf[pr][:], kraw[i4][:], ps_bc[:], ALU.mult), s_v)
                    if j >= 4:
                        b.wait("vector", stk[h % 2],
                               stk_base[h % 2] + 16 * (h // 2))
                    b.run("vector", lambda e, hh=h, tt=th, pr=j % 2:
                          e.tensor_scalar(
                              kbf[hh % 2][:, tt * 512:(tt + 1) * 512],
                              tmpf[pr][:], g_col[:, 0:1], None, ALU.mult), s_v)

            w_re = w_dram.rearrange("(f p) o -> p f o", p=128)
            for p in range(8):
                if p >= 2:
                    b.wait("gpsimd", s_pe, pe_proj[(p - 2) * 8 + 7])
                else:
                    b.wait("gpsimd", s_pe, pe_entry)
                b.dma("gpsimd", wsb[p % 2][:].rearrange("p (f o) -> p f o", f=DC),
                      w_re[:, :, p * 512:(p + 1) * 512], ldw[p % 2])

            for h in range(H):
                b.wait("sync", s_v, v_kbf[2 * h + 1])
                dst = (cck_in[h * 128:(h + 1) * 128, :] if out_kind == "k"
                       else q_dram[h * 128:(h + 1) * 128, :])
                b.dma("sync", dst, kbf[h % 2][:], stk[h % 2])

        emit_proj(wk, bk_sb, gk_sb, "k")
        for s_ in stk:
            b.wait("gpsimd", s_, s_.n)
        b.run("gpsimd", lambda e: e.collective_compute(
            "AllGather", ALU.bypass,
            replica_groups=[[0, 1, 2, 3], [4, 5, 6, 7]],
            ins=[cck_in.opt()], outs=[cck_out.opt()]), s_cc, 1)

        b.wait("tensor", s_s, s_s.n)
        b.wait("vector", s_s, s_s.n)
        v_pe_base, v_v_base = s_pe.n, s_v.n
        v_st_base = [s.n for s in stv]
        v_ld_base = [ldw[0].n, ldw[1].n]
        def vg_pe(gi):
            return v_pe_base + gi + 1
        def vg_v(gi, t2):
            return v_v_base + gi * 2 + t2 + 1
        w_re = wv.rearrange("(f p) o -> p f o", p=128)
        for p in range(8):
            if p >= 2:
                b.wait("gpsimd", s_pe, vg_pe((p - 1) * 4 - 1))
            else:
                b.wait("gpsimd", s_pe, v_pe_base)
            b.dma("gpsimd", wsb[p % 2][:].rearrange("p (f o) -> p f o", f=DC),
                  w_re[:, :, p * 512:(p + 1) * 512], ldw[p % 2])
        for p in range(8):
            b.wait("tensor", ldw[p % 2], v_ld_base[p % 2] + 16 * (p // 2 + 1))
            for tcg in range(4):
                gi = p * 4 + tcg
                if gi >= 1:
                    b.wait("tensor", s_v, vg_v(gi - 1, 1))
                for f in range(DC):
                    for t2 in range(2):
                        tc = tcg * 2 + t2
                        b.run("tensor", lambda e, ff=f, t2_=t2, tc_=tc, pp=p:
                              e.matmul(ps_v2[:, t2_ * 512:(t2_ + 1) * 512],
                                       xnT_v[:, ff, tc_ * 128:(tc_ + 1) * 128],
                                       wsb[pp % 2][:, ff * 512:(ff + 1) * 512],
                                       start=(ff == 0), stop=(ff == DC - 1)),
                              s_pe if (f == DC - 1 and t2 == 1) else None)
                pe_tgt = s_pe.n
                for t2 in range(2):
                    tc = tcg * 2 + t2
                    b.wait("vector", s_pe, pe_tgt)
                    vslot = (gi * 2 + t2) % 4
                    if gi * 2 + t2 >= 4:
                        b.wait("vector", stv[vslot],
                               v_st_base[vslot] + 16 * ((gi * 2 + t2) // 4))
                    b.run("vector", lambda e, sl=vslot, t2_=t2: e.tensor_copy(
                        vpc[sl][:], ps_v2[:, t2_ * 512:(t2_ + 1) * 512]), s_v)
                    b.wait("sync", s_v, s_v.n)
                    off = (p // 2) * 1024 + (p % 2) * 512
                    dst = ccv_in[tc * 512:(tc + 1) * 512, :].rearrange(
                        "(t f) c -> t (f c)", t=128)[:, off:off + 512]
                    b.dma("sync", dst, vpc[vslot][:], stv[vslot])

        for s_ in stv:
            b.wait("gpsimd", s_, s_.n)
        b.run("gpsimd", lambda e: e.collective_compute(
            "AllGather", ALU.bypass,
            replica_groups=[[0, 1, 2, 3], [4, 5, 6, 7]],
            ins=[ccv_in.opt()], outs=[ccv_out.opt()]), s_cc, 1)

        emit_proj(wq, bq_sb, gq_sb, "q")
        q_st_final = [stk[0].n, stk[1].n]
        b.barrier()

    with ExitStack() as sB:
        eB = sB.enter_context
        ao = eB(nc.sbuf_tensor("ao", [128, H * T], BF16))

        with ExitStack() as sB1:
            e4 = sB1.enter_context
            probsT = [e4(nc.sbuf_tensor(f"probsT{i}", [128, DC * 512], BF16))
                      for i in range(2)]
            khb = [e4(nc.sbuf_tensor(f"khb{i}", [128, 4096], BF16))
                   for i in range(2)]
            vhb = [e4(nc.sbuf_tensor(f"vhb{i}", [128, 4096], BF16))
                   for i in range(2)]
            qhb = [e4(nc.sbuf_tensor(f"qhb{i}", [128, 1024], BF16))
                   for i in range(2)]
            recb = [e4(nc.sbuf_tensor(f"recb{i}", [128, 512], F32))
                    for i in range(2)]
            tta = [e4(nc.sbuf_tensor(f"tta{i}", [128, 512], F32))
                   for i in range(2)]
            grow = e4(nc.sbuf_tensor("grow", [1, 256], F32))
            nbcol = e4(nc.sbuf_tensor("nbcol", [128, 1], F32))
            srow2 = e4(nc.sbuf_tensor("srow2", [1, 512], F32))
            rrow2 = e4(nc.sbuf_tensor("rrow2", [1, 512], F32))
            ps_sc = e4(nc.psum_tensor("ps_sc", [128, 2048], F32))
            ps_pv = e4(nc.psum_tensor("ps_pv", [128, 1024], F32))
            ps_sum = e4(nc.psum_tensor("ps_sum", [128, 512], F32))
            ps_m = e4(nc.psum_tensor("ps_m", [128, 512], F32))

            cc_r = cck_out.rearrange("(r z) c -> r z c", r=4)

            b.run("tensor", lambda e: e.matmul(
                ps_m[0:1, 0:128], gq_sb[:, 0:1], idf_in_sb[:],
                start=True, stop=True), s_pe)
            b.run("tensor", lambda e: e.matmul(
                ps_m[0:1, 128:256], gk_sb[:, 0:1], idf_in_sb[:],
                start=True, stop=True), s_pe)
            gt_tgt = s_pe.n
            b.wait("scalar", s_pe, gt_tgt)
            b.run("scalar", lambda e: e.copy(grow[0:1, :], ps_m[0:1, 0:256]), s_s)
            g_cp = s_s.n
            b.wait("vector", s_s, g_cp)
            b.run("vector", lambda e: e.tensor_tensor(
                grow[0:1, :], grow[0:1, :], grow[0:1, :], ALU.mult), s_v)
            b.run("vector", lambda e: e.reduce_max(
                grow[0:1, 0:1], grow[0:1, 0:128], axis=AX), s_v)
            b.run("vector", lambda e: e.reduce_max(
                grow[0:1, 1:2], grow[0:1, 128:256], axis=AX), s_v)
            b.run("vector", lambda e: e.tensor_tensor(
                grow[0:1, 0:1], grow[0:1, 0:1], grow[0:1, 1:2], ALU.mult), s_v)
            g_m2 = s_v.n
            b.wait("scalar", s_v, g_m2)
            b.run("scalar", lambda e: e.activation(
                grow[0:1, 2:3], grow[0:1, 0:1], AF.Sqrt), s_s)
            g_sq = s_s.n
            b.wait("vector", s_s, g_sq)
            b.run("vector", lambda e: e.tensor_scalar(
                grow[0:1, 2:3], grow[0:1, 2:3], -128.0, None, ALU.mult), s_v)
            nb_tgt = s_v.n
            b.wait("tensor", s_v, nb_tgt)
            b.run("tensor", lambda e: e.matmul(
                ps_m[:, 0:1], ones_f[0:1, :], grow[0:1, 2:3],
                start=True, stop=True), s_pe)
            bc_all = s_pe.n
            b.wait("scalar", s_pe, bc_all)
            b.run("scalar", lambda e: e.copy(nbcol[:], ps_m[:, 0:1]), s_s)
            nbcol_tgt = s_s.n

            pe_base = s_pe.n
            def _itbase(it):
                return pe_base + (0 if it == 0 else 16 + (it - 1) * 19)
            def u_sc(u, mcg):
                if u == 0:
                    return pe_base + mcg + 1
                return _itbase(u) + (mcg + 1 if mcg < 15 else 18)
            def u_sums(u):
                return _itbase(u + 1) + (16 if u < 63 else 1)
            def u_pv(u):
                return _itbase(u + 1) + (17 if u < 63 else 2)
            def u_bc(u):
                return _itbase(u + 1) + (19 if u < 63 else 3)
            s_base = s_s.n
            def exp_inc(u, mcg):
                return s_base + u * 16 + mcg + 1
            v_base = s_v.n
            def vsr(u):
                return v_base + u * 5 + 1
            def vrr(u):
                return v_base + u * 5 + 2
            def vre(u):
                return v_base + u * 5 + 3
            def vtt(u):
                return v_base + u * 5 + 4
            def vao(u):
                return v_base + u * 5 + 5
            ldk_base = [ldk[0].n, ldk[1].n]
            ldv_base = [ldv[0].n, ldv[1].n]
            ldq_base = [ldq[0].n, ldq[1].n]

            b.wait("sync", s_cc, s_cc.n)
            b.wait("sync", stk[0], q_st_final[0])
            b.wait("sync", stk[1], q_st_final[1])
            for h in range(H):
                if h >= 2:
                    b.wait("sync", s_pe, u_pv(2 * h - 3))
                b.dma("sync", khb[h % 2][:].rearrange("p (r t) -> p r t", r=4),
                      cc_r[:, h * 128:(h + 1) * 128, :].transpose([1, 0, 2]),
                      ldk[h % 2])
                for r in range(4):
                    off = (h // 8) * 1024 + (h % 8) * 128
                    vsrc = ccv_out[r * D:(r + 1) * D, :].rearrange(
                        "(tt t f) c -> tt t (f c)", tt=TC, t=128)[
                        :, :, off:off + 128].transpose([1, 0, 2])
                    b.dma("sync",
                          vhb[h % 2][:, r * 1024:(r + 1) * 1024].rearrange(
                              "p (tt d) -> p tt d", tt=TC),
                          vsrc, ldv[h % 2])
                b.dma("sync", qhb[h % 2][:], q_dram[h * 128:(h + 1) * 128, :],
                      ldq[h % 2])

            for it in range(65):
                u = it
                if it < 64 and it % 2 == 0:
                    h = it // 2
                    b.wait("tensor", ldk[h % 2],
                           ldk_base[h % 2] + 16 * (h // 2 + 1))
                    b.wait("tensor", ldv[h % 2],
                           ldv_base[h % 2] + 64 * (h // 2 + 1))
                    b.wait("tensor", ldq[h % 2],
                           ldq_base[h % 2] + 16 * (h // 2 + 1))
                for g in range(16):
                    if it >= 1:
                        pu = it - 1
                        ph, pn = divmod(pu, 2)
                        b.wait("tensor", s_s, exp_inc(pu, g))
                        if g == 0 and pu >= 1:
                            b.wait("tensor", s_v, vsr(pu - 1))
                        for j in range(2):
                            mc = 2 * g + j
                            b.run("tensor", lambda e, m_=mc, pb=pu % 2:
                                  e.matmul(
                                      ps_sum[0:1, :], ones_b[:, 0:1],
                                      probsT[pb][:, m_ * 512:(m_ + 1) * 512],
                                      start=(m_ == 0), stop=(m_ == DC - 1)),
                                  s_pe if mc == DC - 1 else None)
                        if g == 0 and pu >= 2:
                            b.wait("tensor", s_v, vtt(pu - 2))
                        for j in range(2):
                            mc = 2 * g + j
                            b.run("tensor", lambda e, hh=ph, m_=mc, pb=pu % 2,
                                  pr=pu % 2: e.matmul(
                                      ps_pv[:, pr * 512:(pr + 1) * 512],
                                      vhb[hh % 2][:, m_ * 128:(m_ + 1) * 128],
                                      probsT[pb][:, m_ * 512:(m_ + 1) * 512],
                                      start=(m_ == 0), stop=(m_ == DC - 1)),
                                  s_pe if mc == DC - 1 else None)
                    if it < 64:
                        h, nh = divmod(u, 2)
                        gidx = u * 16 + g
                        if gidx >= 2:
                            pm = gidx - 2
                            b.wait("tensor", s_s, exp_inc(pm // 16, pm % 16))
                        for j in range(2):
                            mc = 2 * g + j
                            b.run("tensor",
                                  lambda e, hh=h, n_=nh, m_=mc, r=g % 2, jj=j:
                                  e.matmul(ps_sc[:, r * 1024 + jj * 512:
                                                 r * 1024 + (jj + 1) * 512],
                                           khb[hh % 2][:, m_ * 128:(m_ + 1) * 128],
                                           qhb[hh % 2][:, n_ * 512:(n_ + 1) * 512],
                                           start=True, stop=True),
                                  s_pe if j == 1 else None)
                if it >= 1:
                    pu = it - 1
                    b.wait("tensor", s_v, vrr(pu))
                    if pu >= 1:
                        b.wait("tensor", s_v, vre(pu - 1))
                    else:
                        b.wait("tensor", s_s, nbcol_tgt)
                    b.run("tensor", lambda e: e.matmul(
                        ps_m[:], ones_f[0:1, :], rrow2[0:1, :],
                        start=True, stop=True), s_pe)

            for u in range(64):
                for mcg in range(16):
                    b.wait("scalar", s_pe, u_sc(u, mcg))
                    if mcg == 0:
                        if u >= 2:
                            b.wait("scalar", s_pe, u_pv(u - 2))
                        else:
                            b.wait("scalar", s_s, nbcol_tgt)
                    b.run("scalar", lambda e, m_=mcg, pb=u % 2:
                          e.activation(
                              probsT[pb][:, m_ * 1024:(m_ + 1) * 1024],
                              ps_sc[:, (m_ % 2) * 1024:(m_ % 2 + 1) * 1024],
                              AF.Exp, bias=nbcol[:, 0:1]), s_s)

            for u in range(64):
                h, nh = divmod(u, 2)
                b.wait("vector", s_pe, u_sums(u))
                b.run("vector", lambda e: e.tensor_copy(srow2[0:1, :],
                                                        ps_sum[0:1, :]), s_v)
                b.run("vector", lambda e: e.reciprocal(rrow2[0:1, :],
                                                       srow2[0:1, :]), s_v)
                b.wait("vector", s_pe, u_bc(u))
                b.run("vector", lambda e, pb=u % 2: e.tensor_copy(
                    recb[pb][:], ps_m[:]), s_v)
                b.wait("vector", s_pe, u_pv(u))
                b.run("vector", lambda e, pb=u % 2: e.tensor_tensor(
                    tta[pb][:], ps_pv[:, pb * 512:(pb + 1) * 512],
                    recb[pb][:], ALU.mult), s_v)
                b.run("vector", lambda e, hh=h, n_=nh, pb=u % 2: e.tensor_scalar(
                    ao[:, hh * 1024 + n_ * 512:hh * 1024 + (n_ + 1) * 512],
                    tta[pb][:], bv_sb[:, hh:hh + 1], None, ALU.add), s_v)
        b.barrier()

        with ExitStack() as sB2:
            e5 = sB2.enter_context
            wob = [e5(nc.sbuf_tensor(f"wob{i}", [128, DC * 512], BF16))
                   for i in range(2)]
            opan = [e5(nc.sbuf_tensor(f"opan{i}", [128, 4096], BF16))
                    for i in range(2)]
            xsl = [e5(nc.sbuf_tensor(f"xsl{i}", [128, 512], F32))
                   for i in range(4)]
            x1p = [e5(nc.sbuf_tensor(f"x1p{i}", [128, 512], F32))
                   for i in range(4)]
            ps_o = e5(nc.psum_tensor("ps_o", [128, 1024], F32))
            ps_t5 = e5(nc.psum_tensor("ps_t5", [128, 1024], F32))

            pe5, s5, v5 = s_pe.n, s_s.n, s_v.n
            ld5 = [s.n for s in ldx4]
            ldg5 = [ldw[0].n, ldw[1].n]
            st5 = [s.n for s in stx1]
            def o_proj_inc(p, i8):
                return pe5 + p * 16 + i8 + 1
            def o_tq_inc(p, tau):
                return pe5 + p * 16 + 8 + tau + 1
            def o_ev_inc(p, i8):
                return v5 + p * 16 + i8 + 1
            def x1_inc(p, tau):
                return v5 + p * 16 + 8 + tau + 1

            wo_re = wo.rearrange("(f p) o -> p f o", p=128)
            for p in range(8):
                if p >= 2:
                    b.wait("gpsimd", s_pe, o_proj_inc(p - 2, 7))
                b.dma("gpsimd", wob[p % 2][:].rearrange("p (f o) -> p f o", f=DC),
                      wo_re[:, :, p * 512:(p + 1) * 512], ldw[p % 2])
            for p in range(8):
                for tau in range(8):
                    gi = p * 8 + tau
                    if gi >= 4:
                        pj = gi - 4
                        b.wait("sync", s_v, x1_inc(pj // 8, pj % 8))
                    b.dma("sync", xsl[gi % 4][:],
                          x_in[tau * 128:(tau + 1) * 128,
                               p * 512:(p + 1) * 512], ldx4[gi % 4])
            for p in range(8):
                b.wait("tensor", ldw[p % 2], ldg5[p % 2] + 16 * (p // 2 + 1))
                for i8 in range(8):
                    oc, th = divmod(i8, 2)
                    gi8 = p * 8 + i8
                    if gi8 >= 2:
                        pj = gi8 - 2
                        b.wait("tensor", s_v, o_ev_inc(pj // 8, pj % 8))
                    for hc in range(DC):
                        b.run("tensor", lambda e, pp=p, o=oc, tt=th, c=hc, r=i8 % 2:
                              e.matmul(ps_o[:, r * 512:(r + 1) * 512],
                                       wob[pp % 2][:, c * 512 + o * 128:
                                                   c * 512 + (o + 1) * 128],
                                       ao[:, c * 1024 + tt * 512:
                                          c * 1024 + (tt + 1) * 512],
                                       start=(c == 0), stop=(c == DC - 1)),
                              s_pe if hc == DC - 1 else None)
                b.wait("tensor", s_v, o_ev_inc(p, 7))
                for tau in range(8):
                    gq_i = p * 8 + tau
                    if gq_i >= 2:
                        pj = gq_i - 2
                        b.wait("tensor", s_v, x1_inc(pj // 8, pj % 8))
                    for oc in range(4):
                        b.run("tensor", lambda e, pp=p, t_=tau, o=oc: e.matmul(
                            ps_t5[:, (t_ % 2) * 512 + o * 128:
                                  (t_ % 2) * 512 + (o + 1) * 128],
                            opan[pp % 2][:, o * 1024 + t_ * 128:
                                         o * 1024 + (t_ + 1) * 128],
                            ident[:], start=(o == 0), stop=(o == 3)),
                            s_pe if oc == 3 else None)
            for p in range(8):
                for i8 in range(8):
                    oc, th = divmod(i8, 2)
                    b.wait("vector", s_pe, o_proj_inc(p, i8))
                    if p >= 2 and i8 == 0:
                        b.wait("vector", s_pe, o_tq_inc(p - 2, 7))
                    b.run("vector", lambda e, pp=p, o=oc, tt=th, r=i8 % 2:
                          e.tensor_scalar(
                              opan[pp % 2][:, o * 1024 + tt * 512:
                                           o * 1024 + (tt + 1) * 512],
                              ps_o[:, r * 512:(r + 1) * 512],
                              bo_sb[:, pp * 4 + o:pp * 4 + o + 1], None,
                              ALU.add), s_v)
                for tau in range(8):
                    gi = p * 8 + tau
                    b.wait("vector", s_pe, o_tq_inc(p, tau))
                    b.wait("vector", ldx4[gi % 4], ld5[gi % 4] + 16 * (gi // 4 + 1))
                    if gi >= 4:
                        b.wait("vector", stx1[gi % 4], st5[gi % 4] + 16 * (gi // 4))
                    b.run("vector", lambda e, t_=tau, g=gi: e.tensor_tensor(
                        x1p[g % 4][:],
                        ps_t5[:, (t_ % 2) * 512:((t_ % 2) + 1) * 512],
                        xsl[g % 4][:], ALU.add), s_v)
            for p in range(8):
                for tau in range(8):
                    gi = p * 8 + tau
                    b.wait("scalar", s_v, x1_inc(p, tau))
                    b.dma("scalar",
                          x1_dram[tau * 128:(tau + 1) * 128,
                                  p * 512:(p + 1) * 512],
                          x1p[gi % 4][:], stx1[gi % 4])
        b.barrier()
    x1_st_final = [s.n for s in stx1]

    with ExitStack() as sD:
        eD = sD.enter_context
        xn2T = eD(nc.sbuf_tensor("xn2T", [128, DC * T], BF16))

        with ExitStack() as sD1:
            e6 = sD1.enter_context
            sc2 = dict(
                xt=(e6(nc.sbuf_tensor("xt0b", [128, D], F32)),
                    e6(nc.sbuf_tensor("xt1b", [128, D], F32))),
                xsq=e6(nc.sbuf_tensor("xsqb", [128, D], F32)),
                xnb=(e6(nc.sbuf_tensor("xnb0b", [128, D], BF16)),
                     e6(nc.sbuf_tensor("xnb1b", [128, D], BF16))),
                cols=e6(nc.sbuf_tensor("colsb", [128, 8], F32)),
                ps=e6(nc.psum_tensor("ps_a2", [128, 1024], F32)))
            ln_transpose(x1_dram, xn2T, sc2,
                         pre_wait=list(zip(stx1, x1_st_final)))
        b.barrier()

        szc = {}
        for nm in ("sg", "zg", "su", "zu"):
            szc[nm] = [eD(nc.sbuf_tensor(f"{nm}c{i}", [128, 256], F32))
                       for i in range(2)]
        sz_src = dict(sg=sg_in, zg=zg_in, su=su_in, zu=zu_in)
        ldsz = [b.sem("ldsz0"), b.sem("ldsz1")]
        NCH = (ICN + 7) // 8
        ldsz_b = [0, 0]
        def sz_chunk(ch):
            c0 = ch * 256
            cw = min(256, ICN * 32 - c0)
            for nm in ("sg", "zg", "su", "zu"):
                b.dma("scalar", szc[nm][ch % 2][:, 0:cw],
                      sz_src[nm][:, c0:c0 + cw], ldsz[ch % 2])
            b.wait("vector", ldsz[ch % 2], ldsz[ch % 2].n)
            for zn, sn in (("zg", "sg"), ("zu", "su")):
                b.run("vector", lambda e, z=szc[zn][ch % 2], s=szc[sn][ch % 2],
                      w=cw: e.tensor_tensor(z[:, 0:w], z[:, 0:w], s[:, 0:w],
                                            ALU.mult), s_v)
                b.run("vector", lambda e, z=szc[zn][ch % 2], w=cw:
                      e.tensor_scalar(z[:, 0:w], z[:, 0:w], -1.0, None,
                                      ALU.mult), s_v)
            return s_v.n
        nzv = [0] * (NCH + 1)
        sz_chunk(0)
        nzv[0] = s_v.n

        qgb = [eD(nc.sbuf_tensor(f"qgb{i}", [128, D], I32)) for i in range(2)]
        qub = [eD(nc.sbuf_tensor(f"qub{i}", [128, D], I32)) for i in range(2)]
        wgN = eD(nc.sbuf_tensor("wgN", [128, D], BF16))
        wuN = eD(nc.sbuf_tensor("wuN", [128, D], BF16))
        wgT = [eD(nc.sbuf_tensor(f"wgT{i}", [128, D], BF16)) for i in range(2)]
        wuT = [eD(nc.sbuf_tensor(f"wuT{i}", [128, D], BF16)) for i in range(2)]
        gsb = [eD(nc.sbuf_tensor(f"gsb{i}", [128, 512], BF16)) for i in range(2)]
        usb = [eD(nc.sbuf_tensor(f"usb{i}", [128, 512], BF16)) for i in range(2)]
        hac = [eD(nc.sbuf_tensor(f"hac{i}", [128, 1024], BF16)) for i in range(2)]
        ps_g = eD(nc.psum_tensor("ps_g", [128, 1024], F32))
        ps_u = eD(nc.psum_tensor("ps_u", [128, 1024], F32))

        xn2T_v = xn2T[:].rearrange("p (f t) -> p f t", f=DC)
        pe6 = s_pe.n
        def g_inc(ic, th):
            return pe6 + ic * 4 + th + 1
        def u_inc(ic, th):
            return pe6 + ic * 4 + 2 + th + 1
        s6 = s_s.n
        def silu_inc(ic, th):
            return s6 + ic * 4 + 2 * th + 1
        def uev_inc(ic, th):
            return s6 + ic * 4 + 2 * th + 2
        v6 = s_v.n
        _vpos = {}
        _hpos = {}
        _c = v6
        for _ic in range(ICN):
            if _ic % 8 == 0 and _ic // 8 + 1 < NCH:
                _c += 4
            for _hc in range(16):
                _c += 1
                _vpos[(_ic, _hc)] = _c
            if _ic >= 1:
                for _th in range(2):
                    _c += 1
                    _hpos[(_ic - 1, _th)] = _c
            for _hc in range(16, 32):
                _c += 1
                _vpos[(_ic, _hc)] = _c
        def vdq_inc(ic, hc):
            return _vpos[(ic, hc)]
        def hm_inc(ic, th):
            return _hpos[(ic, th)]
        g6 = s_gp.n
        def pdq_inc(ic, hc):
            return g6 + ic * 32 + hc + 1
        ld6g = [ldqg[0].n, ldqg[1].n]
        ld6u = [ldqu[0].n, ldqu[1].n]
        st6 = [sth[0].n, sth[1].n]
        trg_b, tru_b = trg.n, tru.n

        b.dma("sync", qgb[0][:], qwg[0:128, :], ldqg[0])
        b.dma("sync", qub[0][:], qwu[0:128, :], ldqu[0])
        b.dma("sync", qgb[1][:], qwg[128:256, :], ldqg[1])
        b.dma("sync", qub[1][:], qwu[128:256, :], ldqu[1])
        for ic in range(ICN):
            ch = ic // 8
            lc = ic % 8
            if lc == 0 and ch + 1 < NCH:
                if ch >= 1:
                    b.wait("scalar", s_v, vdq_inc(8 * ch - 1, 31))
                    b.wait("scalar", s_gp, pdq_inc(8 * ch - 1, 31))
                sz_chunk(ch + 1)
                nzv[ch + 1] = s_v.n
            b.wait("vector", ldqg[ic % 2], ld6g[ic % 2] + 16 * (ic // 2 + 1))
            b.wait("vector", trg, trg_b + 16 * ic)
            for hc in range(DC):
                if hc == 16 and ic >= 1:
                    for th in range(2):
                        b.wait("vector", s_s, uev_inc(ic - 1, th))
                        if ic >= 3:
                            par_h = (ic - 1) % 2
                            b.wait("vector", sth[par_h],
                                   st6[par_h] + 16 * ((ic - 1) // 2))
                        b.run("vector", lambda e, i=ic - 1, tt=th:
                              e.tensor_tensor(
                                  hac[i % 2][:, tt * 512:(tt + 1) * 512],
                                  gsb[tt][:], usb[tt][:], ALU.mult), s_v)
                b.run("vector", lambda e, i=ic, c=hc, chp=ch % 2, l=lc:
                      e.tensor_scalar(
                          wgN[:, c * 128:(c + 1) * 128],
                          qgb[i % 2][:, c * 128:(c + 1) * 128],
                          szc["sg"][chp][:, l * 32 + c:l * 32 + c + 1],
                          szc["zg"][chp][:, l * 32 + c:l * 32 + c + 1],
                          ALU.mult, ALU.add), s_v)
            b.wait("gpsimd", ldqu[ic % 2], ld6u[ic % 2] + 16 * (ic // 2 + 1))
            b.wait("gpsimd", s_v, nzv[ch])
            b.wait("gpsimd", tru, tru_b + 16 * ic)
            for hc in range(DC):
                b.run("gpsimd", lambda e, i=ic, c=hc, chp=ch % 2, l=lc:
                      e.tensor_scalar(
                          wuN[:, c * 128:(c + 1) * 128],
                          qub[i % 2][:, c * 128:(c + 1) * 128],
                          szc["su"][chp][:, l * 32 + c:l * 32 + c + 1],
                          szc["zu"][chp][:, l * 32 + c:l * 32 + c + 1],
                          ALU.mult, ALU.add), s_gp)
            b.wait("sync", s_v, vdq_inc(ic, 31))
            if ic >= 2:
                b.wait("sync", s_pe, g_inc(ic - 2, 1))
            b.dma("sync", wgT[ic % 2][:].rearrange("p (c q) -> p c q", c=DC),
                  wgN[:].rearrange("p (c q) -> p c q", c=DC), trg,
                  transpose=True)
            b.wait("sync", s_gp, pdq_inc(ic, 31))
            if ic >= 2:
                b.wait("sync", s_pe, u_inc(ic - 2, 1))
            b.dma("sync", wuT[ic % 2][:].rearrange("p (c q) -> p c q", c=DC),
                  wuN[:].rearrange("p (c q) -> p c q", c=DC), tru,
                  transpose=True)
            if ic + 2 < ICN:
                b.dma("sync", qgb[ic % 2][:],
                      qwg[(ic + 2) * 128:(ic + 3) * 128, :], ldqg[ic % 2])
                b.dma("sync", qub[ic % 2][:],
                      qwu[(ic + 2) * 128:(ic + 3) * 128, :], ldqu[ic % 2])
            if ic >= 1:
                b.wait("scalar", s_v, hm_inc(ic - 1, 1))
                b.dma("scalar", hact_dram[(ic - 1) * 128:ic * 128, :],
                      hac[(ic - 1) % 2][:], sth[(ic - 1) % 2])
            b.wait("tensor", trg, trg_b + 16 * (ic + 1))
            for th in range(2):
                if ic >= 1:
                    b.wait("tensor", s_s, silu_inc(ic - 1, th))
                for hc in range(DC):
                    b.run("tensor", lambda e, i=ic, c=hc, tt=th: e.matmul(
                        ps_g[:, tt * 512:(tt + 1) * 512],
                        wgT[i % 2][:, c * 128:(c + 1) * 128],
                        xn2T_v[:, c, tt * 512:(tt + 1) * 512],
                        start=(c == 0), stop=(c == DC - 1)),
                        s_pe if hc == DC - 1 else None)
            b.wait("tensor", tru, tru_b + 16 * (ic + 1))
            for th in range(2):
                if ic >= 1:
                    b.wait("tensor", s_s, uev_inc(ic - 1, th))
                for hc in range(DC):
                    b.run("tensor", lambda e, i=ic, c=hc, tt=th: e.matmul(
                        ps_u[:, tt * 512:(tt + 1) * 512],
                        wuT[i % 2][:, c * 128:(c + 1) * 128],
                        xn2T_v[:, c, tt * 512:(tt + 1) * 512],
                        start=(c == 0), stop=(c == DC - 1)),
                        s_pe if hc == DC - 1 else None)
            for th in range(2):
                b.wait("scalar", s_pe, g_inc(ic, th))
                if ic >= 1:
                    b.wait("scalar", s_v, hm_inc(ic - 1, th))
                b.run("scalar", lambda e, tt=th: e.activation(
                    gsb[tt][:], ps_g[:, tt * 512:(tt + 1) * 512], AF.Silu), s_s)
                b.wait("scalar", s_pe, u_inc(ic, th))
                b.run("scalar", lambda e, tt=th: e.activation(
                    usb[tt][:], ps_u[:, tt * 512:(tt + 1) * 512], AF.Copy), s_s)
        par_t = (ICN - 1) % 2
        b.wait("vector", sth[par_t], st6[par_t] + 16 * ((ICN - 1 - par_t) // 2))
        for th in range(2):
            b.wait("vector", s_s, uev_inc(ICN - 1, th))
            b.run("vector", lambda e, i=ICN - 1, tt=th: e.tensor_tensor(
                hac[i % 2][:, tt * 512:(tt + 1) * 512], gsb[tt][:],
                usb[tt][:], ALU.mult), s_v)
        b.wait("scalar", s_v, s_v.n)
        b.dma("scalar", hact_dram[(ICN - 1) * 128:ICN * 128, :],
              hac[(ICN - 1) % 2][:], sth[(ICN - 1) % 2])
        b.barrier()

    with ExitStack() as sF:
        eF = sF.enter_context
        szd = {}
        for nm, src in (("sd", sd_in), ("zd", zd_in)):
            szd[nm] = eF(nc.sbuf_tensor(nm + "_sb", [128, 32 * ICN], F32))
            b.dma("sync", szd[nm][:], src, ld_sy)
        b.wait("vector", ld_sy)
        b.run("vector", lambda e: e.tensor_tensor(
            szd["zd"][:], szd["zd"][:], szd["sd"][:], ALU.mult), s_v)
        b.run("vector", lambda e: e.tensor_scalar(
            szd["zd"][:], szd["zd"][:], -1.0, None, ALU.mult), s_v)
        nzd_tgt = s_v.n

        qdb = [eF(nc.sbuf_tensor(f"qdb{i}", [128, II], I32)) for i in range(2)]
        wdN = eF(nc.sbuf_tensor("wdN", [128, II], BF16))
        wdc = [eF(nc.sbuf_tensor(f"wdc{i}", [128, II], BF16)) for i in range(2)]

        v_ics = [ic for ic in range(ICN) if ic % 4 != 3]
        p_ics = [ic for ic in range(ICN) if ic % 4 == 3]
        NV, NP = len(v_ics), len(p_ics)
        v7 = s_v.n
        g7 = s_gp.n
        ld7 = [ldqd[0].n, ldqd[1].n]
        st7 = [stw[0].n, stw[1].n]
        trd_b = trd.n

        for hc in range(32):
            if hc >= 2:
                b.wait("scalar", s_v, v7 + (hc - 1) * NV)
                b.wait("scalar", s_gp, g7 + (hc - 1) * NP)
            b.dma("scalar", qdb[hc % 2][:], qwd[hc * 128:(hc + 1) * 128, :],
                  ldqd[hc % 2])
            for eng, ics_e in (("vector", v_ics), ("gpsimd", p_ics)):
                b.wait(eng, ldqd[hc % 2], ld7[hc % 2] + 16 * (hc // 2 + 1))
                if hc == 0:
                    b.wait(eng, s_v, nzd_tgt)
                b.wait(eng, trd, trd_b + 16 * hc)
                sem = s_v if eng == "vector" else s_gp
                for ic in ics_e:
                    b.run(eng, lambda e, h_=hc, i=ic: e.tensor_scalar(
                        wdN[:, i * 128:(i + 1) * 128],
                        qdb[h_ % 2][:, i * 128:(i + 1) * 128],
                        szd["sd"][:, h_ * ICN + i:h_ * ICN + i + 1],
                        szd["zd"][:, h_ * ICN + i:h_ * ICN + i + 1],
                        ALU.mult, ALU.add), sem)
            b.wait("sync", s_v, v7 + hc * NV + NV)
            b.wait("sync", s_gp, g7 + hc * NP + NP)
            if hc >= 2:
                b.wait("sync", stw[hc % 2], st7[hc % 2] + 16 * (hc // 2))
            b.dma("sync", wdc[hc % 2][:].rearrange("p (c q) -> p c q", c=ICN),
                  wdN[:].rearrange("p (c q) -> p c q", c=ICN), trd,
                  transpose=True)
            if hc >= 1:
                b.wait("scalar", trd, trd_b + 16 * hc)
                b.dma("scalar", wdt_dram[hc - 1, :, :], wdc[(hc - 1) % 2][:],
                      stw[(hc - 1) % 2])
        b.wait("scalar", trd, trd.n)
        b.dma("scalar", wdt_dram[31, :, :], wdc[1][:], stw[1])
        b.barrier()
    wdt_st_final = [stw[0].n, stw[1].n]

    with ExitStack() as sG:
        eG = sG.enter_context
        wdts = [eG(nc.sbuf_tensor(f"wdts{i}", [128, 8 * 512], BF16))
                for i in range(2)]
        hsl = [eG(nc.sbuf_tensor(f"hsl{i}", [128, 4 * 512], BF16))
               for i in range(2)]
        dnsb = eG(nc.sbuf_tensor("dnsb", [128, 8 * 512], BF16))
        x1s = [eG(nc.sbuf_tensor(f"x1s{i}", [128, 1024], F32)) for i in range(4)]
        ypc = [eG(nc.sbuf_tensor(f"ypc{i}", [128, 1024], F32)) for i in range(4)]
        ps_d = eG(nc.psum_tensor("ps_d", [128, 8 * 512], F32))

        hact_r = hact_dram.rearrange("(ic p) t -> ic p t", p=128)
        icbs = [list(range(g * 4, min(g * 4 + 4, ICN))) for g in range(22)]
        pe8 = s_pe.n
        def b_inc(pss, bi):
            return pe8 + pss * 30 + bi + 1
        def tq8_inc(pss, tau, half):
            return pe8 + pss * 30 + 22 + tau * 2 + half + 1
        s8 = s_s.n
        def dne_inc(pss, hc8):
            return s8 + pss * 8 + hc8 + 1
        v8 = s_v.n
        def y_inc(pss, tau):
            return v8 + pss * 8 + 2 * tau + 2
        ld8b = [ldb[0].n, ldb[1].n]
        ld8x = ldx1b.n

        b.wait("sync", stw[0], wdt_st_final[0])
        b.wait("sync", stw[1], wdt_st_final[1])
        b.wait("sync", sth[0], sth[0].n)
        b.wait("sync", sth[1], sth[1].n)
        for s_, v_ in zip(stx1, x1_st_final):
            b.wait("sync", s_, v_)
        for pss in range(8):
            th, grp = divmod(pss, 4)
            for bi, ics in enumerate(icbs):
                gi = pss * 22 + bi
                if gi >= 2:
                    pj = gi - 2
                    b.wait("sync", s_pe, b_inc(pj // 22, pj % 22))
                w = len(ics)
                b.dma("sync",
                      wdts[bi % 2][:, 0:8 * w * 128].rearrange(
                          "p (c i) -> p c i", c=8),
                      wdt_dram[grp * 8:(grp + 1) * 8, :,
                               ics[0] * 128:ics[0] * 128 + w * 128]
                      .transpose([1, 0, 2]), ldb[bi % 2])
                b.dma("sync",
                      hsl[bi % 2][:, 0:w * 512].rearrange(
                          "p (i t) -> p i t", i=w),
                      hact_r[ics[0]:ics[0] + w, :, th * 512:(th + 1) * 512]
                      .transpose([1, 0, 2]), ldb[bi % 2])
            for tau in range(4):
                if pss >= 1:
                    b.wait("sync", s_v, y_inc(pss - 1, tau))
                b.dma("sync", x1s[tau][:],
                      x1_dram[th * 512 + tau * 128:th * 512 + (tau + 1) * 128,
                              grp * 1024:(grp + 1) * 1024], ldx1b)
            for bi, ics in enumerate(icbs):
                gb = pss * 22 + bi
                b.wait("tensor", ldb[bi % 2], ld8b[bi % 2] + 32 * (gb // 2 + 1))
                if bi == 0 and pss >= 1:
                    b.wait("tensor", s_v, y_inc(pss - 1, 3))
                w = len(ics)
                for j, ic in enumerate(ics):
                    for hc8 in range(8):
                        b.run("tensor", lambda e, b_=bi, jj=j, c8=hc8, i=ic,
                              w_=w:
                              e.matmul(ps_d[:, c8 * 512:(c8 + 1) * 512],
                                       wdts[b_ % 2][:, c8 * w_ * 128 + jj * 128:
                                                    c8 * w_ * 128 + (jj + 1) * 128],
                                       hsl[b_ % 2][:, jj * 512:(jj + 1) * 512],
                                       start=(i == 0), stop=(i == ICN - 1)),
                              s_pe if (j == w - 1 and hc8 == 7) else None)
            for hc8 in range(8):
                b.wait("scalar", s_pe, b_inc(pss, 21))
                b.run("scalar", lambda e, c8=hc8: e.copy(
                    dnsb[:, c8 * 512:(c8 + 1) * 512],
                    ps_d[:, c8 * 512:(c8 + 1) * 512]), s_s)
            for tau in range(4):
                for half in range(2):
                    b.wait("tensor", s_s, dne_inc(pss, 7))
                    if tau * 2 + half >= 2:
                        b.wait("tensor", s_v, y_inc(pss, tau - 1))
                    for j in range(4):
                        hc8 = half * 4 + j
                        b.run("tensor", lambda e, c8=hc8, jj=j, hf=half, t_=tau:
                              e.matmul(ps_d[:, hf * 512 + jj * 128:
                                            hf * 512 + (jj + 1) * 128],
                                       dnsb[:, c8 * 512 + t_ * 128:
                                            c8 * 512 + (t_ + 1) * 128],
                                       ident[:], start=(jj == 0), stop=(jj == 3)),
                              s_pe if j == 3 else None)
                b.wait("vector", s_pe, tq8_inc(pss, tau, 1))
                b.wait("vector", ldx1b, ld8x + 64 * (pss + 1))
                if pss >= 1:
                    b.wait("vector", sty4[tau], 16 * pss)
                for half in range(2):
                    b.run("vector", lambda e, t_=tau, hf=half: e.tensor_tensor(
                        ypc[t_][:, hf * 512:(hf + 1) * 512],
                        ps_d[:, hf * 512:(hf + 1) * 512],
                        x1s[t_][:, hf * 512:(hf + 1) * 512], ALU.add), s_v)
                b.wait("scalar", s_v, y_inc(pss, tau))
                b.dma("scalar",
                      y_out[th * 512 + tau * 128:th * 512 + (tau + 1) * 128,
                            grp * 1024:(grp + 1) * 1024],
                      ypc[tau][:], sty4[tau])
        for s_ in sty4:
            b.wait("sync", s_, s_.n)
        b.barrier(drain=False)

    b.finalize_block()
    b.ctx.close()
    return nc


_NC_CACHE = None


def kernel(**inputs):
    global _NC_CACHE
    xs = np.ascontiguousarray(np.asarray(inputs["L0_attn_hidden_states"]))
    ident = np.eye(128, dtype=ml_dtypes.bfloat16)
    ones_b = np.ones((128, 128), dtype=ml_dtypes.bfloat16)
    ones_f = np.ones((128, 128), dtype=np.float32)

    def pcol(v):
        return np.ascontiguousarray(np.asarray(v, np.float32).reshape(32, 128).T)

    def prow_gu(v):
        return np.ascontiguousarray(
            np.asarray(v, np.float32).reshape(ICN, 128, 32)
            .transpose(1, 0, 2).reshape(128, -1))

    def prow_d(v):
        return np.ascontiguousarray(
            np.asarray(v, np.float32).reshape(32, 128, ICN)
            .transpose(1, 0, 2).reshape(128, -1))

    common = dict(
        wq=np.ascontiguousarray(np.asarray(inputs["L0_attn_Wq"], np.float32)),
        wk=np.ascontiguousarray(np.asarray(inputs["L0_attn_Wk"], np.float32)),
        wv=np.ascontiguousarray(np.asarray(inputs["L0_attn_Wv"], np.float32)),
        wo=np.ascontiguousarray(np.asarray(inputs["L0_attn_Wo"], np.float32)),
        qwg=np.ascontiguousarray(np.asarray(inputs["L0_ffn_qw_gate"], np.int32)),
        qwu=np.ascontiguousarray(np.asarray(inputs["L0_ffn_qw_up"], np.int32)),
        qwd=np.ascontiguousarray(np.asarray(inputs["L0_ffn_qw_down"], np.int32)),
        sg=prow_gu(inputs["L0_ffn_s_gate"]), zg=prow_gu(inputs["L0_ffn_z_gate"]),
        su=prow_gu(inputs["L0_ffn_s_up"]), zu=prow_gu(inputs["L0_ffn_z_up"]),
        sd=prow_d(inputs["L0_ffn_s_down"]), zd=prow_d(inputs["L0_ffn_z_down"]),
        bq_p=pcol(inputs["L0_attn_bq"]), bk_p=pcol(inputs["L0_attn_bk"]),
        bv_p=pcol(inputs["L0_attn_bv"]), bo_p=pcol(inputs["L0_attn_bo"]),
        gq_c=np.ascontiguousarray(
            np.asarray(inputs["L0_attn_gq"], np.float32).reshape(128, 1)),
        gk_c=np.ascontiguousarray(
            np.asarray(inputs["L0_attn_gk"], np.float32).reshape(128, 1)),
        ident_bf=ident, ones_bf=ones_b, ones_f=ones_f,
        ident_f=np.eye(128, dtype=np.float32),
    )
    in_maps = []
    for c in range(8):
        bb, r = divmod(c, 4)
        m = dict(common)
        m["x_in"] = np.ascontiguousarray(xs[bb, r * T:(r + 1) * T, :], np.float32)
        in_maps.append(m)

    if _NC_CACHE is None:
        _NC_CACHE = build_program()
    res = run_bass_kernel_spmd(_NC_CACHE, in_maps, core_ids=list(range(8)))
    out = np.empty((B, N, D), np.float32)
    for c in range(8):
        bb, r = divmod(c, 4)
        out[bb, r * T:(r + 1) * T, :] = res.results[c]["y"]
    return out
